# version 13
# baseline (speedup 1.0000x reference)
import numpy as np
import ml_dtypes
import concourse.bass as bass
import concourse.mybir as mybir
from concourse.bass_utils import run_bass_kernel_spmd
from contextlib import ExitStack

F32 = mybir.dt.float32
BF16 = mybir.dt.bfloat16
AF = mybir.ActivationFunctionType
ALU = mybir.AluOpType
AX = mybir.AxisListType
AP = bass.AP
NPBF = ml_dtypes.bfloat16

D = 4096
S = 2048
HK, H, DH = 8, 32, 128
NIC = 32
EPS = 1e-5
ISQ = float(1.0 / np.sqrt(128.0))
NCP = [8, 16]
PERMS = [[0, 1, 3, 2], [1, 0, 2, 3]]


class Sem:
    def __init__(self, handle, name):
        self.h, self.n, self.name = handle, 0, name


_WRAP_V = {"tensor_copy", "tensor_tensor", "tensor_add", "tensor_sub", "tensor_mul",
           "tensor_max", "tensor_scalar", "tensor_scalar_add", "tensor_scalar_sub",
           "tensor_scalar_mul", "tensor_scalar_max", "tensor_scalar_min",
           "tensor_single_scalar", "reduce_sum", "reduce_max", "tensor_reduce",
           "reciprocal", "scalar_tensor_tensor", "memset", "select", "tensor_relu"}
_WRAP_S = {"activation", "copy", "add", "mul", "memzero"}


class SelfSync:

    def __init__(self, build, eng, sem, names):
        self._b, self._e, self._s, self._names = build, eng, sem, names

    def __getattr__(self, name):
        f = getattr(self._e, name)
        if name not in self._names:
            return f
        def wrap(*a, **k):
            s = self._s
            if s.n > 0:
                self._e.wait_ge(s.h, s.n)
                self._b.waits.append((s, s.n))
            ins = f(*a, **k)
            ins.then_inc(s.h, 1)
            s.n += 1
            return ins
        return wrap


class Build:
    def __init__(self, taps=(), stop_after="full", marks=None):
        self.taps = set(taps)
        self.stop_after = stop_after
        self.rec = marks is None
        self.marks = {} if marks is None else marks
        nc = bass.Bass("TRN2", target_bir_lowering=False)
        self.nc = nc
        self.stack = ExitStack()
        self.waits = []

        dt = lambda n, s, d=F32, k="ExternalInput": nc.dram_tensor(n, s, d, kind=k).ap()
        self.x_h = dt("x", [S, D])
        self.wq_h = dt("wq_t", [D, D])
        self.wkv_h = dt("wkv_t", [D, 512])
        self.wku_h = dt("wku_t", [256, 1024])
        self.wvu_h = dt("wvu_t", [256, 1024])
        self.wo_h = dt("wo_t", [D, D])
        self.wb_h = dt("wb_t", [D, D])
        self.at_h = dt("at_t", [D, 128])
        self.bt_h = dt("bt_t", [128, D])
        self.wr_h = dt("wr_t", [D, 8])
        self.bb_h = dt("bb", [D])
        self.ident_h = dt("cst_ident", [128, 128], BF16)
        self.masksA_h = dt("cst_masksA", [128, 4096], BF16)
        self.masksB_h = dt("cst_masksB", [128, 4096], BF16)
        self.e8_h = dt("cst_e8", [8, 128], BF16)
        self.onesf_h = dt("cst_ones_f32", [128, 128], F32)
        self.onesb_h = dt("cst_ones_bf", [128, 128], BF16)
        self.out_h = dt("out", [1024, D], F32, "ExternalOutput")
        self.tap_h = {}
        for t in self.taps:
            shp = {"kcvcT": [512, S], "latT": [512, S], "KT": [1024, S],
                   "V": [S, 1024], "QT": [4096, 512], "attn_outT": [4096, 512],
                   "x2": [512, D], "x1T": [4096, 512], "B": [128, 8],
                   "x3T": [4096, 512], "cb": [128, 512], "hw": [128, 512],
                   "fU": [512, D], "logit": [128, 48]}[t]
            self.tap_h[t] = dt("tap_" + t, shp, F32, "ExternalOutput")

        sb = lambda n, s, d: self.stack.enter_context(nc.sbuf_tensor(n, s, d))
        self.KT = sb("KT", [128, 16384], BF16)
        self.V = sb("V", [128, 16384], BF16)
        self.x2 = sb("x2", [128, 16384], BF16)
        self.a1 = sb("a1", [128, 16384], BF16)
        self.a2 = sb("a2", [128, 16384], BF16)
        self.a3 = sb("a3", [128, 4096], F32)
        self.wbuf = sb("wbuf", [128, 2048], BF16)
        self.wsmall = sb("wsmall", [128, 4096], BF16)
        self.ident = sb("ident", [128, 128], BF16)
        self.masks = sb("masks", [128, 4096], BF16)
        self.onesf = sb("onesf", [128, 128], F32)
        self.onesb = sb("onesb", [128, 128], BF16)
        self.e8 = sb("e8", [8, 128], BF16)
        self.wr_bf = sb("wr_bf", [128, 256], BF16)
        self.probs = sb("probs", [128, 2048], BF16)
        self.recb = sb("recb", [128, 512], F32)
        self.cb = sb("cb", [128, 512], BF16)
        self.hw = sb("hw", [128, 512], BF16)
        self.stats = sb("stats", [128, 96], F32)
        self.sm = sb("sm", [128, 64], F32)
        self.logit = sb("logit", [128, 48], F32)
        self.logitb = sb("logitb", [128, 32], BF16)
        self.combT = sb("combT", [8, 512], BF16)
        self.bvec = sb("bvec", [128, 1], F32)

        ps = lambda n, s, d: self.stack.enter_context(nc.psum_tensor(n, s, d))
        self.pb = [ps(f"pb{i}", [128, 512], F32) for i in range(6)]
        self.pf = [ps(f"pf{i}", [128, 1024], BF16) for i in range(2)]

        mk = lambda n: Sem(self.stack.enter_context(nc.semaphore(n)), n)
        self.s_x = mk("s_x")
        self.s_x1 = mk("s_x1")
        self.s_c = mk("s_c")
        self.s_t = mk("s_t")
        self.s_t1 = mk("s_t1")
        self.s_we = mk("s_we")
        self.s_wo = mk("s_wo")
        self.s_w2 = mk("s_w2")
        self.s_w3 = mk("s_w3")
        self.s_w4 = mk("s_w4")
        self.s_w5 = mk("s_w5")
        self.s_w6 = mk("s_w6")
        self.s_w7 = mk("s_w7")
        self.s_ws = mk("s_ws")
        self.s_mm = mk("s_mm")
        self.s_st = mk("s_st")
        self.s_dv = mk("s_dv")
        self.s_ds = mk("s_ds")
        self.semmap = {s.name: s for s in
                       [self.s_x, self.s_x1, self.s_c, self.s_t, self.s_t1, self.s_we, self.s_wo,
                        self.s_w2, self.s_w3, self.s_w4, self.s_w5, self.s_w6,
                        self.s_w7,
                        self.s_ws, self.s_mm, self.s_st, self.s_dv, self.s_ds]}
        self.block = self.stack.enter_context(nc.Block())

    def wsem(self, idx, depth=4):
        return [self.s_we, self.s_wo, self.s_w2, self.s_w3,
                self.s_w4, self.s_w5, self.s_w6, self.s_w7][idx % depth]

    def wchunk_buf(self, idx, depth=4):
        k = idx % depth
        if k < 4:
            return self.wbuf[:, 512 * k:512 * (k + 1)]
        if depth == 6:
            return self.probs[:, 1024 + 512 * (k - 4):1024 + 512 * (k - 3)]
        return self.wsmall[:, 512 * (k - 4):512 * (k - 3)]

    def mark(self, name, sem):
        if self.rec:
            assert name not in self.marks, f"dup mark {name}"
            self.marks[name] = (sem.name, sem.n)
        else:
            assert self.marks[name] == (sem.name, sem.n), \
                f"mark {name}: {self.marks[name]} != {(sem.name, sem.n)}"

    def wtm(self, eng, name):
        if self.rec:
            return
        semname, val = self.marks[name]
        sem = self.semmap[semname]
        self.waits.append((sem, val))
        eng.wait_ge(sem.h, val)

    def wt(self, eng, sem, val=None):
        if val is None:
            val = sem.n
        self.waits.append((sem, val))
        eng.wait_ge(sem.h, val)

    def inc(self, instr, sem, k=1):
        instr.then_inc(sem.h, k)
        sem.n += k
        return sem.n

    def finish(self):
        if not self.rec:
            for sem, val in self.waits:
                assert val <= sem.n, f"wait {sem.name}>={val} but total {sem.n}"
        self.stack.close()
        return self.nc

    def emit_consts(self):
        b = self
        @b.block.sync
        def _(sync):
            for dst, src in [(b.ident, b.ident_h), (b.onesf, b.onesf_h),
                             (b.onesb, b.onesb_h), (b.e8, b.e8_h)]:
                b.inc(sync.dma_start(out=dst[:], in_=src[:]), b.s_c, 16)
            b.mark("consts", b.s_c)
        @b.block.gpsimd
        def _(g):
            b.inc(g.dma_start(
                out=AP(b.wr_bf[:].tensor, 0, [[256, 128], [8, 32], [1, 8]]),
                in_=AP(b.wr_h.tensor, 0, [[8, 128], [1024, 32], [1, 8]])), b.s_ws, 16)
            b.mark("wr", b.s_ws)

    def emit_prologue(self):
        b = self
        XS = 8192
        SQ = 12288
        x2t = b.x2[:].tensor
        a1t = b.a1[:].tensor
        a2t = b.a2[:].tensor
        P1 = [(t_lo, Thi, g) for t_lo in range(8) for Thi in range(2) for g in range(4)]
        P2 = [(u, Thi, hHi, h4) for u in range(4) for Thi in range(2)
              for hHi in range(2) for h4 in range(4)]

        @b.block.sync
        def _(sync):
            ktt = b.KT[:].tensor
            def dmat(tt):
                b.wtm(sync, f"pro:x1w:{tt}")
                if tt % 4 == 0 and tt >= 8:
                    b.wtm(sync, f"pro:mmkv:{tt // 4 - 2}:{NIC - 1}")
                stg_ = XS if tt % 2 == 0 else 0
                st_ = b.s_t if tt % 2 == 0 else b.s_t1
                dstt = a1t if (tt // 4) % 2 == 0 else ktt
                b.inc(sync.dma_start_transpose(
                    out=AP(dstt, 128 * (tt % 4), [[16384, 128], [512, 32], [1, 128]]),
                    in_=AP(x2t, stg_, [[16384, 128], [128, 32], [1, 128]])),
                    st_, 16)
                b.mark(f"pro:dmat:{tt}", st_)
            for t in range(16):
                if t >= 1:
                    b.wtm(sync, f"pro:xfree:{t - 1}:0")
                    b.wtm(sync, f"pro:xfreed:{t - 1}:0")
                b.inc(sync.dma_start(
                    out=b.a3[:, 0:2048],
                    in_=b.x_h[128 * t:128 * (t + 1), 0:2048]),
                    b.s_x, 16)
                b.mark(f"pro:x:{t}:0", b.s_x)
                if t == 0:
                    b.inc(sync.dma_start(
                        out=b.a3[:, 2048:4096],
                        in_=b.x_h[0:128, 2048:4096]),
                        b.s_x1, 16)
                    b.mark("pro:x:0:1", b.s_x1)
                if t >= 1:
                    dmat(t - 1)
            dmat(15)

        @b.block.gpsimd
        def _(g):
            for srcw, base in [(b.wku_h, 0), (b.wvu_h, 2048)]:
                for jc in range(2):
                    b.inc(g.dma_start(out=b.wsmall[:, base + 1024 * jc:base + 1024 * (jc + 1)],
                                      in_=srcw[128 * jc:128 * (jc + 1), :]), b.s_ws, 16)
            b.mark("pro:wup", b.s_ws)
            for s in range(4):
                for ic in range(NIC):
                    idx = s * NIC + ic
                    if idx >= 4:
                        pg, pi = divmod(idx - 4, NIC)
                        b.wtm(g, f"pro:mmkv:{pg}:{pi}")
                    buf = (idx % 4) * 512
                    sem = b.wsem(idx)
                    b.inc(g.dma_start(out=b.wbuf[:, buf:buf + 512],
                                      in_=b.wkv_h[128 * ic:128 * (ic + 1), :]), sem, 16)
                    b.mark(f"pro:wkv:{s}:{ic}", sem)

        @b.block.scalar
        def _(sc):
            sc = SelfSync(b, sc, b.s_ds, _WRAP_S)
            def sq(t):
                sqc = 8 + 2 * t
                b.wtm(sc, f"pro:x:{t}:0")
                sc.activation(out=AP(x2t, SQ, [[16384, 128], [1, 2048]]),
                              in_=b.a3[:, 0:2048], func=AF.Square,
                              accum_out=b.sm[:, sqc:sqc + 1])
                b.mark(f"pro:xfree:{t}:0", b.s_ds)
                b.wtm(sc, f"pro:x:{t}:1")
                sc.activation(out=AP(x2t, SQ + 2048, [[16384, 128], [1, 2048]]),
                              in_=b.a3[:, 2048:4096], func=AF.Square,
                              accum_out=b.sm[:, sqc + 1:sqc + 2])
                b.mark(f"pro:xfree:{t}:1", b.s_ds)
                b.mark(f"pro:sq:{t}", b.s_ds)
                if t + 1 < 16:
                    b.wtm(sc, f"pro:xfreed:{t}:1")
                    b.wt(sc, b.s_ds)
                    b.inc(sc.dma_start(
                        out=b.a3[:, 2048:4096],
                        in_=b.x_h[128 * (t + 1):128 * (t + 2), 2048:4096]),
                        b.s_x1, 16)
                    b.mark(f"pro:x:{t + 1}:1", b.s_x1)
            def sqr(t):
                mc = 64 + 2 * t
                b.wtm(sc, f"pro:var:{t}")
                sc.activation(out=b.stats[:, mc + 1:mc + 2],
                              in_=b.stats[:, mc + 1:mc + 2], func=AF.Sqrt)
                b.mark(f"pro:sqrt:{t}", b.s_ds)
            sq(0)
            for t in range(1, 16):
                sqr(t - 1)
                sq(t)
            sqr(15)
            P1_ = [(t_lo, Thi, g) for t_lo in range(8) for Thi in range(2) for g in range(4)]
            P2_ = [(u, Thi, hHi, h4) for u in range(4) for Thi in range(2)
                   for hHi in range(2) for h4 in range(4)]
            for i, (t_lo, Thi, g) in enumerate(P1_):
                if i % 2 != 1:
                    continue
                b.wtm(sc, f"pro:scr1:{i}")
                psx = b.pf[i % 2]
                srcp = AP(psx[:].tensor, 0, [[1024, 128], [32, 4], [1, 32]])
                dstp = AP(x2t, 4096 * Thi + 512 * t_lo + 256 * (g // 2) + 4 * (g % 2),
                          [[16384, 128], [1, 4], [8, 32]])
                sc.copy(out=dstp, in_=srcp)
                b.mark(f"pro:scr1e:{i}", b.s_ds)
            for i, (u, Thi, hHi, h4) in enumerate(P2_):
                if (64 + i) % 2 != 1:
                    continue
                b.wtm(sc, f"pro:scr2:{i}")
                psx = b.pf[(64 + i) % 2]
                srcp = AP(psx[:].tensor, 0, [[1024, 128], [16, 8], [1, 16]])
                dstp = AP(a2t, 8192 + 2048 * u + 1024 * hHi + 256 * h4 + 128 * Thi,
                          [[16384, 128], [16, 8], [1, 16]])
                sc.copy(out=dstp, in_=srcp)
                b.mark(f"pro:scr2e:{i}", b.s_ds)
            for i in range(32):
                h2, rb = i // 4, i % 4
                b.wtm(sc, f"pro:kte:{h2}:{rb}")
                if i >= 2:
                    b.wtm(sc, f"pro:kn:{i - 2}")
                half = (i % 2) * 512
                sc.activation(out=AP(x2t, SQ + half, [[16384, 128], [1, 512]]),
                              in_=b.KT[:, 2048 * h2 + 512 * rb:2048 * h2 + 512 * (rb + 1)],
                              func=AF.Square)
                b.mark(f"pro:ksq:{i}", b.s_ds)

        @b.block.tensor
        def _(te):
            b.wtm(te, "consts")
            ktt2 = b.KT[:].tensor
            for s in range(4):
                b.wtm(te, f"pro:dmat:{4 * s + 2}")
                b.wtm(te, f"pro:dmat:{4 * s + 3}")
                if s >= 1:
                    b.wtm(te, f"pro:kcvce:{s - 1}")
                stript = a1t if s % 2 == 0 else ktt2
                for ic in range(NIC):
                    b.wtm(te, f"pro:wkv:{s}:{ic}")
                    buf = ((s * NIC + ic) % 4) * 512
                    for fc in range(4):
                        ins = te.matmul(out=b.pb[fc][:],
                                        lhsT=b.wbuf[:, buf + 128 * fc:buf + 128 * (fc + 1)],
                                        rhs=AP(stript, 512 * ic, [[16384, 128], [1, 512]]),
                                        start=(ic == 0), stop=(ic == NIC - 1))
                    b.inc(ins, b.s_mm)
                    b.mark(f"pro:mmkv:{s}:{ic}", b.s_mm)
            for i, (t_lo, Thi, g) in enumerate(P1):
                if i == 0:
                    b.wtm(te, "pro:kcvce:3")
                if i >= 2:
                    b.wtm(te, f"pro:scr1e:{i - 2}")
                psx = b.pf[i % 2]
                in_ap = AP(a2t, 2048 * g + 1024 * Thi + t_lo, [[16384, 128], [128, 8], [8, 16]])
                b.inc(te.transpose(out=psx[:, 0:128], in_=in_ap, identity=b.ident[:]), b.s_mm)
                b.mark(f"pro:scr1:{i}", b.s_mm)
            for i, (u, Thi, hHi, h4) in enumerate(P2):
                if i == 0:
                    b.wtm(te, "pro:scr1e:62")
                    b.wtm(te, "pro:scr1e:63")
                if i >= 2:
                    b.wtm(te, f"pro:scr2e:{i - 2}")
                psx = b.pf[(64 + i) % 2]
                in_ap = AP(x2t, 4096 * Thi + 1024 * u + 4 * hHi + h4,
                           [[16384, 128], [512, 2], [256, 2], [8, 32]])
                b.inc(te.transpose(out=psx[:, 0:128], in_=in_ap, identity=b.ident[:]), b.s_mm)
                b.mark(f"pro:scr2:{i}", b.s_mm)
            b.wtm(te, "pro:wup")
            for h2 in range(8):
                for rb in range(4):
                    i = 4 * h2 + rb
                    if i == 0:
                        b.wtm(te, "pro:scr2e:62")
                        b.wtm(te, "pro:scr2e:63")
                    if i >= 2:
                        b.wtm(te, f"pro:kte:{(i - 2) // 4}:{(i - 2) % 4}")
                    psU = b.pb[4 + i % 2]
                    for jc in range(2):
                        ins = te.matmul(out=psU[:],
                                        lhsT=b.wsmall[:, 1024 * jc + 128 * h2:1024 * jc + 128 * (h2 + 1)],
                                        rhs=AP(a2t, 8192 + 2048 * jc + 512 * rb, [[16384, 128], [1, 512]]),
                                        start=(jc == 0), stop=(jc == 1))
                    b.inc(ins, b.s_mm)
                    b.mark(f"pro:mmku:{h2}:{rb}", b.s_mm)
            for rc in range(16):
                for vg in range(2):
                    i = 2 * rc + vg
                    if i < 2:
                        b.wtm(te, f"pro:kte:7:{2 + i}")
                    else:
                        b.wtm(te, f"pro:ve:{(i - 2) // 2}:{(i - 2) % 2}")
                    psV = b.pb[4 + i % 2]
                    for jc in range(2):
                        ins = te.matmul(out=psV[:],
                                        lhsT=AP(a2t, 8192 + 2048 * (2 + jc) + 128 * rc,
                                                [[16384, 128], [1, 128]]),
                                        rhs=b.wsmall[:, 2048 + 1024 * jc + 512 * vg:2048 + 1024 * jc + 512 * (vg + 1)],
                                        start=(jc == 0), stop=(jc == 1))
                    b.inc(ins, b.s_mm)
                    b.mark(f"pro:mmv:{rc}:{vg}", b.s_mm)
            for i in range(32):
                b.wtm(te, f"pro:ksq:{i}")
                if i == 0:
                    b.wtm(te, "pro:ve:15:0")
                if i >= 1:
                    b.wtm(te, f"pro:kn:{i - 1}")
                half = (i % 2) * 512
                b.inc(te.matmul(out=b.pb[4][0:1, :], lhsT=b.onesb[:, 0:1],
                                rhs=AP(x2t, SQ + half, [[16384, 128], [1, 512]]),
                                start=True, stop=True), b.s_mm)
                b.mark(f"pro:mmkn:{i}", b.s_mm)

        @b.block.vector
        def _(v):
            v = SelfSync(b, v, b.s_dv, _WRAP_V)
            for t in range(16):
                sc0 = 40 + 2 * (t % 4)
                sqc = 8 + 2 * t
                mc = 64 + 2 * t
                stg = XS if t % 2 == 0 else 0
                b.wtm(v, f"pro:x:{t}:0")
                if t >= 2:
                    b.wtm(v, f"pro:dmat:{t - 2}")
                if t >= 4:
                    b.wtm(v, f"pro:var:{t - 4}")
                v.tensor_scalar(out=AP(x2t, stg, [[16384, 128], [1, 2048]]),
                                in0=b.a3[:, 0:2048],
                                scalar1=0.0, scalar2=0.0, op0=ALU.add,
                                op1=ALU.add,
                                accum_out=b.sm[:, sc0:sc0 + 1])
                b.mark(f"pro:xfreed:{t}:0", b.s_dv)
                b.wtm(v, f"pro:x:{t}:1")
                v.tensor_scalar(out=AP(x2t, stg + 2048, [[16384, 128], [1, 2048]]),
                                in0=b.a3[:, 2048:4096],
                                scalar1=0.0, scalar2=0.0, op0=ALU.add,
                                op1=ALU.add,
                                accum_out=b.sm[:, sc0 + 1:sc0 + 2])
                b.mark(f"pro:xfreed:{t}:1", b.s_dv)
                b.wtm(v, f"pro:sq:{t}")
                v.tensor_tensor(out=b.sm[:, 56:57], in0=b.sm[:, sc0:sc0 + 1],
                                in1=b.sm[:, sc0 + 1:sc0 + 2], op=ALU.add)
                v.tensor_scalar(out=b.stats[:, mc:mc + 1], in0=b.sm[:, 56:57],
                                scalar1=1.0 / D, scalar2=None, op0=ALU.mult)
                v.tensor_tensor(out=b.sm[:, 43:44], in0=b.sm[:, sqc:sqc + 1],
                                in1=b.sm[:, sqc + 1:sqc + 2], op=ALU.add)
                v.tensor_scalar(out=b.sm[:, 43:44], in0=b.sm[:, 43:44],
                                scalar1=1.0 / D, scalar2=EPS, op0=ALU.mult, op1=ALU.add)
                v.tensor_tensor(out=b.sm[:, 44:45], in0=b.stats[:, mc:mc + 1],
                                in1=b.stats[:, mc:mc + 1], op=ALU.mult)
                v.tensor_tensor(out=b.sm[:, 43:44], in0=b.sm[:, 43:44], in1=b.sm[:, 44:45],
                                op=ALU.subtract)
                v.reciprocal(out=b.stats[:, mc + 1:mc + 2], in_=b.sm[:, 43:44])
                b.mark(f"pro:var:{t}", b.s_dv)
                b.wtm(v, f"pro:sqrt:{t}")
                for hf in range(2):
                    v.tensor_scalar(out=AP(x2t, stg + 2048 * hf, [[16384, 128], [1, 2048]]),
                                    in0=AP(x2t, stg + 2048 * hf, [[16384, 128], [1, 2048]]),
                                    scalar1=b.stats[:, mc:mc + 1],
                                    scalar2=b.stats[:, mc + 1:mc + 2],
                                    op0=ALU.subtract, op1=ALU.mult)
                b.mark(f"pro:x1w:{t}", b.s_dv)
                evs = {7: [0], 11: [1], 15: [2, 3]}.get(t, [])
                for s in evs:
                    b.wtm(v, f"pro:mmkv:{s}:{NIC - 1}")
                    for fc in range(4):
                        v.tensor_copy(out=AP(a2t, 2048 * fc + 512 * s, [[16384, 128], [1, 512]]),
                                      in_=b.pb[fc][:])
                    b.mark(f"pro:kcvce:{s}", b.s_dv)
            for i, (t_lo, Thi, g) in enumerate(P1):
                if i % 2 != 0:
                    continue
                b.wtm(v, f"pro:scr1:{i}")
                psx = b.pf[i % 2]
                src = AP(psx[:].tensor, 0, [[1024, 128], [32, 4], [1, 32]])
                dst = AP(x2t, 4096 * Thi + 512 * t_lo + 256 * (g // 2) + 4 * (g % 2),
                         [[16384, 128], [1, 4], [8, 32]])
                v.tensor_copy(out=dst, in_=src)
                b.mark(f"pro:scr1e:{i}", b.s_dv)
            for i, (u, Thi, hHi, h4) in enumerate(P2):
                if (64 + i) % 2 != 0:
                    continue
                b.wtm(v, f"pro:scr2:{i}")
                psx = b.pf[(64 + i) % 2]
                src = AP(psx[:].tensor, 0, [[1024, 128], [16, 8], [1, 16]])
                dst = AP(a2t, 8192 + 2048 * u + 1024 * hHi + 256 * h4 + 128 * Thi,
                         [[16384, 128], [16, 8], [1, 16]])
                v.tensor_copy(out=dst, in_=src)
                b.mark(f"pro:scr2e:{i}", b.s_dv)
            for h2 in range(8):
                for rb in range(4):
                    b.wtm(v, f"pro:mmku:{h2}:{rb}")
                    if h2 == 0 and rb == 0:
                        b.wtm(v, f"pro:mmkv:3:{NIC - 1}")
                    psU = b.pb[4 + (4 * h2 + rb) % 2]
                    v.tensor_copy(out=b.KT[:, 2048 * h2 + 512 * rb:2048 * h2 + 512 * (rb + 1)],
                                  in_=psU[:])
                    b.mark(f"pro:kte:{h2}:{rb}", b.s_dv)
            for rc in range(16):
                for vg in range(2):
                    b.wtm(v, f"pro:mmv:{rc}:{vg}")
                    psV = b.pb[4 + (2 * rc + vg) % 2]
                    v.tensor_copy(out=b.V[:, 1024 * rc + 512 * vg:1024 * rc + 512 * (vg + 1)],
                                  in_=psV[:])
                    b.mark(f"pro:ve:{rc}:{vg}", b.s_dv)
            for i in range(32):
                b.wtm(v, f"pro:mmkn:{i}")
                v.reduce_max(out=b.sm[0:1, 59:60], in_=b.pb[4][0:1, :], axis=AX.X)
                if i == 0:
                    v.tensor_copy(out=b.sm[0:1, 57:58], in_=b.sm[0:1, 59:60])
                else:
                    v.tensor_tensor(out=b.sm[0:1, 57:58], in0=b.sm[0:1, 57:58],
                                    in1=b.sm[0:1, 59:60], op=ALU.max)
                b.mark(f"pro:kn:{i}", b.s_dv)

    def emit_pass(self, P):
        b = self
        NC = NCP[P]
        x2t = b.x2[:].tensor
        a1t = b.a1[:].tensor
        a2t = b.a2[:].tensor
        pp = f"P{P}"
        pass_dv0 = b.s_dv.n
        pass_mm0 = b.s_mm.n
        pass_st0 = b.s_st.n

        @b.block.sync
        def _(sync):
            b.wt(sync, b.s_dv, pass_dv0)
            if P >= 1:
                b.wt(sync, b.s_st, pass_st0)
            b.inc(sync.dma_start(out=b.masks[:],
                                 in_=(b.masksA_h if P == 0 else b.masksB_h)[:]), b.s_c, 16)
            b.mark(f"{pp}i:masks", b.s_c)
            def dmat(tt):
                b.wtm(sync, f"{pp}i:x1w:{tt}")
                st_ = b.s_t if tt % 2 == 0 else b.s_t1
                b.inc(sync.dma_start_transpose(
                    out=AP(a1t, 128 * tt, [[16384, 128], [512, 32], [1, 128]]),
                    in_=AP(a2t, 4096 * (tt % 2), [[16384, 128], [128, 32], [1, 128]])),
                    st_, 16)
                b.mark(f"{pp}i:dmat:{tt}", st_)
            for t in range(4):
                if t >= 1:
                    b.wtm(sync, f"{pp}i:xfreed:{t - 1}:0")
                b.inc(sync.dma_start(
                    out=b.a3[:, 0:2048],
                    in_=b.x_h[1024 * P + 128 * t:1024 * P + 128 * (t + 1), 0:2048]),
                    b.s_x, 16)
                b.mark(f"{pp}i:x:{t}:0", b.s_x)
                if t == 0:
                    b.inc(sync.dma_start(
                        out=b.a3[:, 2048:4096],
                        in_=b.x_h[1024 * P:1024 * P + 128, 2048:4096]),
                        b.s_x1, 16)
                    b.mark(f"{pp}i:x:0:1", b.s_x1)
                if t >= 1:
                    dmat(t - 1)
            dmat(3)

        @b.block.gpsimd
        def _(g):
            for gg in range(8):
                for ic in range(NIC):
                    idx = gg * NIC + ic
                    if idx < 8:
                        if P == 0:
                            b.wtm(g, ("pro:mmkv:3:%d" % (28 + idx)) if idx < 4
                                  else "pro:mmku:7:3")
                        else:
                            b.wtm(g, ("P0f:mmfb:7:%d" % (28 + idx)) if idx < 4
                                  else "P0f:mmh")
                    else:
                        pg, pi = divmod(idx - 8, NIC)
                        b.wtm(g, f"{pp}i:mmq:{pg}:{pi}")
                    sem = b.wsem(idx, 8)
                    b.inc(g.dma_start(out=b.wchunk_buf(idx, 8),
                                      in_=b.wq_h[128 * ic:128 * (ic + 1), 512 * gg:512 * (gg + 1)]),
                          sem, 16)
                    b.mark(f"{pp}i:wq:{gg}:{ic}", sem)

        @b.block.scalar
        def _(sc):
            sc = SelfSync(b, sc, b.s_ds, _WRAP_S)
            for t in range(1, 4):
                b.wtm(sc, f"{pp}i:xfreed:{t - 1}:1")
                b.wt(sc, b.s_ds)
                b.inc(sc.dma_start(
                    out=b.a3[:, 2048:4096],
                    in_=b.x_h[1024 * P + 128 * t:1024 * P + 128 * (t + 1), 2048:4096]),
                    b.s_x1, 16)
                b.mark(f"{pp}i:x:{t}:1", b.s_x1)
            for i in range(32):
                g = i // 4
                b.wtm(sc, f"{pp}i:qte:{g}")
                if i >= 2:
                    b.wtm(sc, f"{pp}i:qn:{i - 2}")
                half = (i % 2) * 512
                sc.activation(out=AP(x2t, half, [[16384, 128], [1, 512]]),
                              in_=AP(a2t, 512 * i, [[16384, 128], [1, 512]]),
                              func=AF.Square)
                b.mark(f"{pp}i:qsq:{i}", b.s_ds)
            b.wtm(sc, f"{pp}i:bsq")
            sc.activation(out=b.sm[0:1, 58:59], in_=b.sm[0:1, 58:59], func=AF.Sqrt)
            b.mark(f"{pp}i:bsqrt", b.s_ds)

        @b.block.vector
        def _(v):
            v = SelfSync(b, v, b.s_dv, _WRAP_V)
            for t in range(4):
                stg = 4096 * (t % 2)
                mc = 64 + 16 * P + 2 * t
                b.wtm(v, f"pro:sqrt:{8 * P + t}")
                b.wtm(v, f"{pp}i:x:{t}:0")
                if t >= 2:
                    b.wtm(v, f"{pp}i:dmat:{t - 2}")
                v.tensor_scalar(out=AP(a2t, stg, [[16384, 128], [1, 2048]]),
                                in0=b.a3[:, 0:2048],
                                scalar1=b.stats[:, mc:mc + 1],
                                scalar2=b.stats[:, mc + 1:mc + 2],
                                op0=ALU.subtract, op1=ALU.mult)
                b.mark(f"{pp}i:xfreed:{t}:0", b.s_dv)
                b.wtm(v, f"{pp}i:x:{t}:1")
                v.tensor_scalar(out=AP(a2t, stg + 2048, [[16384, 128], [1, 2048]]),
                                in0=b.a3[:, 2048:4096],
                                scalar1=b.stats[:, mc:mc + 1],
                                scalar2=b.stats[:, mc + 1:mc + 2],
                                op0=ALU.subtract, op1=ALU.mult)
                b.mark(f"{pp}i:xfreed:{t}:1", b.s_dv)
                b.mark(f"{pp}i:x1w:{t}", b.s_dv)
            for g in range(8):
                b.wtm(v, f"{pp}i:mmq:{g}:{NIC - 1}")
                for hcl in range(4):
                    hc = 4 * g + hcl
                    v.tensor_copy(out=AP(a2t, 512 * hc, [[16384, 128], [1, 512]]),
                                  in_=b.pb[hcl][:])
                b.mark(f"{pp}i:qte:{g}", b.s_dv)
            for i in range(32):
                b.wtm(v, f"{pp}i:mmqn:{i}")
                v.reduce_max(out=b.sm[0:1, 59:60], in_=b.pb[4][0:1, :], axis=AX.X)
                if i == 0:
                    v.tensor_copy(out=b.sm[0:1, 56:57], in_=b.sm[0:1, 59:60])
                else:
                    v.tensor_tensor(out=b.sm[0:1, 56:57], in0=b.sm[0:1, 56:57],
                                    in1=b.sm[0:1, 59:60], op=ALU.max)
                b.mark(f"{pp}i:qn:{i}", b.s_dv)
            v.tensor_tensor(out=b.sm[0:1, 58:59], in0=b.sm[0:1, 56:57],
                            in1=b.sm[0:1, 57:58], op=ALU.mult)
            b.mark(f"{pp}i:bsq", b.s_dv)
            b.wtm(v, f"{pp}i:bsqrt")
            v.tensor_scalar(out=b.sm[0:1, 58:59], in0=b.sm[0:1, 58:59],
                            scalar1=-ISQ, scalar2=None, op0=ALU.mult)
            v.tensor_copy(out=b.recb[0:1, 0:1], in_=b.sm[0:1, 58:59])
            b.mark(f"{pp}i:negB", b.s_dv)
            b.wtm(v, f"{pp}i:mmB")
            v.tensor_copy(out=b.bvec[:], in_=b.pb[5][:, 0:1])
            b.mark(f"{pp}i:B", b.s_dv)

        @b.block.tensor
        def _(te):
            b.wt(te, b.s_dv, pass_dv0)
            for g in range(8):
                if g == 0:
                    b.wtm(te, f"{pp}i:dmat:2")
                    b.wtm(te, f"{pp}i:dmat:3")
                if g >= 1:
                    b.wtm(te, f"{pp}i:qte:{g - 1}")
                for ic in range(NIC):
                    b.wtm(te, f"{pp}i:wq:{g}:{ic}")
                    wb_ = b.wchunk_buf(g * NIC + ic, 8)
                    for hcl in range(4):
                        ins = te.matmul(out=b.pb[hcl][:],
                                        lhsT=wb_[:, 128 * hcl:128 * (hcl + 1)],
                                        rhs=AP(a1t, 512 * ic, [[16384, 128], [1, 512]]),
                                        start=(ic == 0), stop=(ic == NIC - 1))
                    b.inc(ins, b.s_mm)
                    b.mark(f"{pp}i:mmq:{g}:{ic}", b.s_mm)
            for i in range(32):
                b.wtm(te, f"{pp}i:qsq:{i}")
                if i >= 1:
                    b.wtm(te, f"{pp}i:qn:{i - 1}")
                half = (i % 2) * 512
                b.inc(te.matmul(out=b.pb[4][0:1, :], lhsT=b.onesb[:, 0:1],
                                rhs=AP(x2t, half, [[16384, 128], [1, 512]]),
                                start=True, stop=True), b.s_mm)
                b.mark(f"{pp}i:mmqn:{i}", b.s_mm)
            b.wtm(te, f"{pp}i:negB")
            b.inc(te.matmul(out=b.pb[5][:, 0:1], lhsT=b.onesf[0:1, :],
                            rhs=b.recb[0:1, 0:1], start=True, stop=True), b.s_mm)
            b.mark(f"{pp}i:mmB", b.s_mm)

        if self.stop_after == "i":
            return

        chunks = []
        for u in range(H // 2):
            for kc in range(NC):
                for hp in range(2):
                    chunks.append((2 * u + hp, kc))
        jlast = {}
        for j, (h, kc) in enumerate(chunks):
            if kc == NC - 1:
                jlast[h] = j
        slot_of_chunk = {}
        slot_of_rb = {}
        SB = [0, 1]

        @b.block.tensor
        def _(te):
            prev_user = {0: None, 1: None}
            slot_ctr2 = [0]

            def emit_scores(j):
                h, kc = chunks[j]
                h2 = h // 4
                if j == 0:
                    b.wtm(te, f"{pp}i:qte:7")
                sl = slot_ctr2[0] % 2
                slot_ctr2[0] += 1
                slot_of_chunk[j] = sl
                if prev_user[sl] is not None:
                    b.wtm(te, prev_user[sl])
                prev_user[sl] = f"{pp}ii:exp:{j}"
                ins = te.matmul(out=b.pb[SB[sl]][:],
                                lhsT=b.KT[:, 2048 * h2 + 128 * kc:2048 * h2 + 128 * (kc + 1)],
                                rhs=AP(a2t, 512 * h, [[16384, 128], [1, 512]]),
                                start=True, stop=True)
                b.inc(ins, b.s_mm)
                b.mark(f"{pp}ii:mmsc:{j}", b.s_mm)

            def emit_tail(j):
                h, kc = chunks[j]
                h2 = h // 4
                if kc >= NC - 8:
                    b.wtm(te, f"{pp}ii:mask:{j}")
                else:
                    b.wtm(te, f"{pp}ii:exp:{j}")
                if kc == 0:
                    if h >= 2:
                        b.wtm(te, f"{pp}ii:norm:{h - 2}")
                        b.wtm(te, f"{pp}ii:recip:{h - 2}")
                    if h == 0:
                        b.wtm(te, f"{pp}i:qn:31")
                    if h == 1:
                        b.wtm(te, f"{pp}i:B")
                te.matmul(out=b.pb[2 + h % 2][:],
                          lhsT=b.V[:, 1024 * kc + 128 * h2:1024 * kc + 128 * (h2 + 1)],
                          rhs=b.probs[:, 512 * (j % 4):512 * (j % 4 + 1)],
                          start=(kc == 0), stop=(kc == NC - 1))
                ins = te.matmul(out=b.pb[4 + h % 2][0:1, :], lhsT=b.onesb[:, 0:1],
                                rhs=b.probs[:, 512 * (j % 4):512 * (j % 4 + 1)],
                                start=(kc == 0), stop=(kc == NC - 1))
                b.inc(ins, b.s_mm)
                b.mark(f"{pp}ii:mmsu:{j}", b.s_mm)
                if kc == NC - 1:
                    b.wtm(te, f"{pp}ii:recip:{h}")
                    sl = slot_ctr2[0] % 2
                    slot_ctr2[0] += 1
                    slot_of_rb[h] = sl
                    if prev_user[sl] is not None:
                        b.wtm(te, prev_user[sl])
                    prev_user[sl] = f"{pp}ii:recb:{h}"
                    b.inc(te.matmul(out=b.pb[SB[sl]][:], lhsT=b.onesf[0:1, :],
                                    rhs=b.recb[0:1, :], start=True, stop=True), b.s_mm)
                    b.mark(f"{pp}ii:mmrb:{h}", b.s_mm)

            NCH = len(chunks)
            for j in range(NCH + 2):
                if j < NCH:
                    emit_scores(j)
                if j >= 2:
                    emit_tail(j - 2)

        @b.block.scalar
        def _(sc):
            sc = SelfSync(b, sc, b.s_ds, _WRAP_S)
            for j, (h, kc) in enumerate(chunks):
                if j == 0:
                    b.wtm(sc, f"{pp}i:B")
                b.wtm(sc, f"{pp}ii:mmsc:{j}")
                if j >= 4:
                    b.wtm(sc, f"{pp}ii:mmsu:{j - 4}")
                sc.activation(out=b.probs[:, 512 * (j % 4):512 * (j % 4 + 1)],
                              in_=b.pb[SB[slot_of_chunk[j]]][:], func=AF.Exp,
                              bias=b.bvec[:], scale=ISQ)
                b.mark(f"{pp}ii:exp:{j}", b.s_ds)

        @b.block.vector
        def _(v):
            v = SelfSync(b, v, b.s_dv, _WRAP_V)
            first_mask = [True]
            for j, (h, kc) in enumerate(chunks):
                if kc >= NC - 8:
                    if first_mask[0]:
                        b.wtm(v, f"{pp}i:masks")
                        first_mask[0] = False
                    dj = kc - (NC - 8)
                    b.wtm(v, f"{pp}ii:exp:{j}")
                    v.tensor_tensor(out=b.probs[:, 512 * (j % 4):512 * (j % 4 + 1)],
                                    in0=b.probs[:, 512 * (j % 4):512 * (j % 4 + 1)],
                                    in1=b.masks[:, 512 * dj:512 * (dj + 1)],
                                    op=ALU.mult)
                    b.mark(f"{pp}ii:mask:{j}", b.s_dv)
                if kc == NC - 1:
                    b.wtm(v, f"{pp}ii:mmsu:{j}")
                    v.reciprocal(out=b.recb[0:1, :], in_=b.pb[4 + h % 2][0:1, :])
                    b.mark(f"{pp}ii:recip:{h}", b.s_dv)
                    b.wtm(v, f"{pp}ii:mmrb:{h}")
                    v.tensor_copy(out=b.recb[:], in_=b.pb[SB[slot_of_rb[h]]][:])
                    b.mark(f"{pp}ii:recb:{h}", b.s_dv)
                    v.tensor_tensor(out=AP(a1t, 512 * h, [[16384, 128], [1, 512]]),
                                    in0=b.pb[2 + h % 2][:], in1=b.recb[:],
                                    op=ALU.mult)
                    b.mark(f"{pp}ii:norm:{h}", b.s_dv)

        if self.stop_after == "ii":
            return

        mm_ii_last = b.s_mm.n

        @b.block.gpsimd
        def _(g):
            for og in range(8):
                for hc in range(NIC):
                    idx = og * NIC + hc
                    if idx < 8:
                        b.wtm(g, f"{pp}i:mmq:7:{24 + idx}")
                    else:
                        pg, pi = divmod(idx - 8, NIC)
                        b.wtm(g, f"{pp}o:mmo:{pg}:{pi}")
                    sem = b.wsem(idx, 8)
                    b.inc(g.dma_start(out=b.wchunk_buf(idx, 8),
                                      in_=b.wo_h[128 * hc:128 * (hc + 1), 512 * og:512 * (og + 1)]),
                          sem, 16)
                    b.mark(f"{pp}o:wo:{og}:{hc}", sem)

        @b.block.tensor
        def _(te):
            for og in range(8):
                if og >= 1:
                    b.wtm(te, f"{pp}o:aue:{og - 1}")
                for hc in range(NIC):
                    b.wtm(te, f"{pp}o:wo:{og}:{hc}")
                    if og == 0 and hc == 0:
                        b.wtm(te, f"{pp}ii:norm:{H - 1}")
                    wb_ = b.wchunk_buf(og * NIC + hc, 8)
                    for qc in range(4):
                        ins = te.matmul(out=b.pb[qc][:],
                                        lhsT=AP(a1t, 512 * hc + 128 * qc, [[16384, 128], [1, 128]]),
                                        rhs=wb_,
                                        start=(hc == 0), stop=(hc == NIC - 1))
                    b.inc(ins, b.s_mm)
                    b.mark(f"{pp}o:mmo:{og}:{hc}", b.s_mm)

        @b.block.vector
        def _(v):
            v = SelfSync(b, v, b.s_dv, _WRAP_V)
            for og in range(8):
                b.wtm(v, f"{pp}o:mmo:{og}:{NIC - 1}")
                for qc in range(4):
                    v.tensor_copy(out=AP(a2t, 4096 * qc + 512 * og, [[16384, 128], [1, 512]]),
                                  in_=b.pb[qc][:])
                b.mark(f"{pp}o:aue:{og}", b.s_dv)
            for qc in range(4):
                b.wtm(v, f"{pp}o:ausq:{qc}")
                v.reduce_sum(out=b.sm[:, 48 + qc:49 + qc],
                             in_=b.sm[:, 8 + 8 * qc:16 + 8 * qc], axis=AX.X)
                v.reduce_sum(out=b.sm[:, 52 + qc:53 + qc],
                             in_=b.stats[:, 8 * qc:8 * qc + 8], axis=AX.X)
                v.tensor_scalar(out=b.stats[:, 32 + 2 * qc:33 + 2 * qc],
                                in0=b.sm[:, 48 + qc:49 + qc],
                                scalar1=1.0 / D, scalar2=None, op0=ALU.mult)
                v.tensor_scalar(out=b.sm[:, 16 + qc:17 + qc], in0=b.sm[:, 52 + qc:53 + qc],
                                scalar1=1.0 / D, scalar2=EPS, op0=ALU.mult, op1=ALU.add)
                v.tensor_tensor(out=b.sm[:, 20 + qc:21 + qc], in0=b.stats[:, 32 + 2 * qc:33 + 2 * qc],
                                in1=b.stats[:, 32 + 2 * qc:33 + 2 * qc], op=ALU.mult)
                v.tensor_tensor(out=b.sm[:, 16 + qc:17 + qc], in0=b.sm[:, 16 + qc:17 + qc],
                                in1=b.sm[:, 20 + qc:21 + qc], op=ALU.subtract)
                v.reciprocal(out=b.stats[:, 33 + 2 * qc:34 + 2 * qc],
                             in_=b.sm[:, 16 + qc:17 + qc])
                b.mark(f"{pp}o:var:{qc}", b.s_dv)
                b.wtm(v, f"{pp}o:sqrt:{qc}")
                b.wtm(v, f"{pp}o:x:{qc}")
                for hf in range(2):
                    v.tensor_scalar(out=AP(a1t, 4096 * qc + 2048 * hf, [[16384, 128], [1, 2048]]),
                                    in0=AP(a2t, 4096 * qc + 2048 * hf, [[16384, 128], [1, 2048]]),
                                    scalar1=b.stats[:, 32 + 2 * qc:33 + 2 * qc],
                                    scalar2=b.stats[:, 33 + 2 * qc:34 + 2 * qc],
                                    op0=ALU.subtract, op1=ALU.mult)
                    v.tensor_tensor(out=AP(x2t, 4096 * qc + 2048 * hf, [[16384, 128], [1, 2048]]),
                                    in0=AP(a1t, 4096 * qc + 2048 * hf, [[16384, 128], [1, 2048]]),
                                    in1=b.a3[:, 2048 * hf:2048 * (hf + 1)], op=ALU.add)
                b.mark(f"{pp}o:x2:{qc}", b.s_dv)
            b.mark(f"{pp}o:x2done", b.s_dv)

        @b.block.scalar
        def _(sc):
            sc = SelfSync(b, sc, b.s_ds, _WRAP_S)
            for og in range(8):
                b.wtm(sc, f"{pp}o:aue:{og}")
                for qc in range(4):
                    sl = AP(a2t, 4096 * qc + 512 * og, [[16384, 128], [1, 512]])
                    sc.activation(out=AP(x2t, 0, [[16384, 128], [1, 512]]),
                                  in_=sl, func=AF.Copy,
                                  accum_out=b.sm[:, 8 + 8 * qc + og:9 + 8 * qc + og])
                    sc.activation(out=AP(x2t, 512, [[16384, 128], [1, 512]]),
                                  in_=sl, func=AF.Square,
                                  accum_out=b.stats[:, 8 * qc + og:8 * qc + og + 1])
                    if og == 7:
                        b.mark(f"{pp}o:ausq:{qc}", b.s_ds)
            b.mark(f"{pp}o:ausq", b.s_ds)
            for qc in range(4):
                b.wtm(sc, f"{pp}o:var:{qc}")
                sc.activation(out=b.stats[:, 33 + 2 * qc:34 + 2 * qc],
                              in_=b.stats[:, 33 + 2 * qc:34 + 2 * qc], func=AF.Sqrt)
                b.mark(f"{pp}o:sqrt:{qc}", b.s_ds)

        @b.block.sync
        def _(sync):
            for qc in range(4):
                for hf in range(2):
                    idx = 2 * qc + hf
                    if idx >= 2 and hf == 0:
                        b.wtm(sync, f"{pp}o:x2:{qc - 1}")
                    b.inc(sync.dma_start(
                        out=b.a3[:, 2048 * hf:2048 * (hf + 1)],
                        in_=b.x_h[1024 * P + 128 * qc:1024 * P + 128 * (qc + 1),
                                  2048 * hf:2048 * (hf + 1)]),
                        b.s_x, 16)
                b.mark(f"{pp}o:x:{qc}", b.s_x)

        if self.stop_after == "iii":
            return

        self.emit_ffn(P)

    def emit_ffn(self, P):
        b = self
        x2t = b.x2[:].tensor
        a1t = b.a1[:].tensor
        a2t = b.a2[:].tensor
        pp = f"P{P}"
        mm_iii_last = b.s_mm.n

        @b.block.sync
        def _(sync):
            for tc in range(4):
                b.wtm(sync, f"{pp}f:x3w:{tc}")
                b.inc(sync.dma_start_transpose(
                    out=AP(a1t, 128 * tc, [[16384, 128], [512, 32], [1, 128]]),
                    in_=AP(a2t, 0, [[16384, 128], [128, 32], [1, 128]])),
                    b.s_t, 16)
                b.mark(f"{pp}f:dmat:{tc}", b.s_t)
            for tc in range(4):
                for hf in range(2):
                    b.wtm(sync, f"{pp}f:out:{tc}:{hf}")
                    b.inc(sync.dma_start(
                        out=b.out_h[512 * P + 128 * tc:512 * P + 128 * (tc + 1),
                                    2048 * hf:2048 * (hf + 1)],
                        in_=b.a3[:, 2048 * hf:2048 * (hf + 1)]),
                        b.s_st, 16)
                    b.mark(f"{pp}f:st:{tc}:{hf}", b.s_st)

        @b.block.gpsimd
        def _(g):
            b.wtm(g, f"{pp}o:mmo:7:{NIC - 1}")
            b.inc(g.dma_start(out=AP(b.wsmall[:].tensor, 0, [[4096, 128], [128, 32], [1, 128]]),
                              in_=AP(b.at_h.tensor, 0, [[128, 128], [16384, 32], [1, 128]])),
                  b.s_ws, 16)
            b.mark(f"{pp}f:wat", b.s_ws)
            NCH_ = H * NCP[P]
            for og in range(8):
                if og == 0:
                    b.wtm(g, f"{pp}ii:mmsu:{NCH_ - 1}")
                if og >= 1:
                    b.wtm(g, f"{pp}f:mmf:{og - 1}")
                b.inc(g.dma_start(out=b.probs[:, 0:512],
                                  in_=b.bt_h[:, 512 * og:512 * (og + 1)]), b.s_ws, 16)
                b.inc(g.dma_start(out=b.probs[0:1, 512:1024],
                                  in_=b.bb_h[None, 512 * og:512 * (og + 1)]), b.s_ws, 16)
                b.mark(f"{pp}f:wbt:{og}", b.s_ws)
                for ic in range(NIC):
                    idx = og * NIC + ic
                    if idx < 6:
                        b.wtm(g, f"{pp}o:mmo:7:{28 + idx if idx < 4 else 24 + idx}")
                    else:
                        pg, pi = divmod(idx - 6, NIC)
                        b.wtm(g, f"{pp}f:mmfb:{pg}:{pi}")
                    sem = b.wsem(idx, 6)
                    b.inc(g.dma_start(out=b.wchunk_buf(idx, 6),
                                      in_=b.wb_h[128 * ic:128 * (ic + 1), 512 * og:512 * (og + 1)]),
                          sem, 16)
                    b.mark(f"{pp}f:wb:{og}:{ic}", sem)

        @b.block.vector
        def _(v):
            v = SelfSync(b, v, b.s_dv, _WRAP_V)
            for tc in range(4):
                b.wtm(v, f"{pp}f:x3s:{tc}")
                v.tensor_scalar(out=b.stats[:, 40 + 2 * tc:41 + 2 * tc],
                                in0=b.sm[:, 40 + tc:41 + tc],
                                scalar1=1.0 / D, scalar2=None, op0=ALU.mult)
                v.tensor_scalar(out=b.sm[:, 28 + tc:29 + tc], in0=b.sm[:, 44 + tc:45 + tc],
                                scalar1=1.0 / D, scalar2=EPS, op0=ALU.mult, op1=ALU.add)
                v.tensor_tensor(out=b.sm[:, 36 + tc:37 + tc], in0=b.stats[:, 40 + 2 * tc:41 + 2 * tc],
                                in1=b.stats[:, 40 + 2 * tc:41 + 2 * tc], op=ALU.mult)
                v.tensor_tensor(out=b.sm[:, 28 + tc:29 + tc], in0=b.sm[:, 28 + tc:29 + tc],
                                in1=b.sm[:, 36 + tc:37 + tc], op=ALU.subtract)
                v.reciprocal(out=b.stats[:, 41 + 2 * tc:42 + 2 * tc],
                             in_=b.sm[:, 28 + tc:29 + tc])
                b.mark(f"{pp}f:x3var:{tc}", b.s_dv)
                b.wtm(v, f"{pp}f:x3sqrt:{tc}")
                if tc >= 1:
                    b.wtm(v, f"{pp}f:dmat:{tc - 1}")
                v.tensor_scalar(out=AP(a2t, 0, [[16384, 128], [1, 4096]]),
                                in0=AP(x2t, 4096 * tc, [[16384, 128], [1, 4096]]),
                                scalar1=b.stats[:, 40 + 2 * tc:41 + 2 * tc],
                                scalar2=b.stats[:, 41 + 2 * tc:42 + 2 * tc],
                                op0=ALU.subtract, op1=ALU.mult)
                b.mark(f"{pp}f:x3w:{tc}", b.s_dv)
            b.mark(f"{pp}f:x3var", b.s_dv)
            for tc in range(4):
                b.wtm(v, f"{pp}f:mmL:{tc}")
                v.tensor_copy(out=b.logit[:, 8 * tc:8 * (tc + 1)], in_=b.pb[4][:, 0:8])
                b.mark(f"{pp}f:logit:{tc}", b.s_dv)
            for tc in range(4):
                lg = b.logit[:, 8 * tc:8 * (tc + 1)]
                v.reduce_max(out=b.sm[:, 8 + tc:9 + tc], in_=lg, axis=AX.X)
                v.tensor_scalar(out=b.logit[:, 32:40], in0=lg,
                                scalar1=b.sm[:, 8 + tc:9 + tc], scalar2=None, op0=ALU.is_equal)
                v.scalar_tensor_tensor(out=b.logit[:, 40:48], in0=b.logit[:, 32:40],
                                       scalar=-1e9, in1=lg, op0=ALU.mult, op1=ALU.add)
                v.reduce_max(out=b.sm[:, 12 + tc:13 + tc], in_=b.logit[:, 40:48], axis=AX.X)
                v.tensor_tensor(out=b.sm[:, 20 + tc:21 + tc], in0=b.sm[:, 8 + tc:9 + tc],
                                in1=b.sm[:, 12 + tc:13 + tc], op=ALU.subtract)
                b.mark(f"{pp}f:d21:{tc}", b.s_dv)
                b.wtm(v, f"{pp}f:sig:{tc}")
                v.tensor_scalar(out=b.logit[:, 40:48], in0=b.logit[:, 40:48],
                                scalar1=b.sm[:, 12 + tc:13 + tc], scalar2=None, op0=ALU.is_equal)
                v.tensor_scalar(out=b.sm[:, 16 + tc:17 + tc], in0=b.sm[:, 20 + tc:21 + tc],
                                scalar1=-1.0, scalar2=1.0, op0=ALU.mult, op1=ALU.add)
                v.tensor_scalar(out=b.logit[:, 32:40], in0=b.logit[:, 32:40],
                                scalar1=b.sm[:, 20 + tc:21 + tc], scalar2=None, op0=ALU.mult)
                v.tensor_scalar(out=b.logit[:, 40:48], in0=b.logit[:, 40:48],
                                scalar1=b.sm[:, 16 + tc:17 + tc], scalar2=None, op0=ALU.mult)
                v.tensor_tensor(out=b.logit[:, 32:40], in0=b.logit[:, 32:40],
                                in1=b.logit[:, 40:48], op=ALU.add)
                v.tensor_copy(out=b.logitb[:, 8 * tc:8 * (tc + 1)], in_=b.logit[:, 32:40])
                b.mark(f"{pp}f:comb:{tc}", b.s_dv)
            for tc in range(4):
                b.wtm(v, f"{pp}f:mmct:{tc}")
                psx = b.pf[tc % 2]
                v.tensor_copy(out=b.combT[0:8, 128 * tc:128 * (tc + 1)],
                              in_=psx[0:8, 0:128])
                b.mark(f"{pp}f:ctev:{tc}", b.s_dv)
            b.mark(f"{pp}f:combT", b.s_dv)
            b.wtm(v, f"{pp}f:mmcb")
            v.tensor_copy(out=b.cb[:], in_=b.pb[5][:])
            b.mark(f"{pp}f:cb", b.s_dv)
            b.wtm(v, f"{pp}f:mmh")
            v.tensor_tensor(out=b.hw[:], in0=b.pb[4][:], in1=b.cb[:], op=ALU.mult)
            b.mark(f"{pp}f:hw", b.s_dv)
            for og in range(8):
                b.wtm(v, f"{pp}f:mmf:{og}")
                for tc in range(4):
                    v.tensor_copy(out=AP(a2t, 4096 * tc + 512 * og, [[16384, 128], [1, 512]]),
                                  in_=b.pb[tc][:])
                b.mark(f"{pp}f:fue:{og}", b.s_dv)
            b.wtm(v, f"{pp}f:fsq")
            for tc in range(4):
                v.reduce_sum(out=b.sm[:, 48 + tc:49 + tc],
                             in_=b.sm[:, 8 + 8 * tc:16 + 8 * tc], axis=AX.X)
                v.reduce_sum(out=b.sm[:, 52 + tc:53 + tc],
                             in_=b.stats[:, 8 * tc:8 * tc + 8], axis=AX.X)
            for tc in range(4):
                v.tensor_scalar(out=b.stats[:, 48 + 2 * tc:49 + 2 * tc],
                                in0=b.sm[:, 48 + tc:49 + tc],
                                scalar1=1.0 / D, scalar2=None, op0=ALU.mult)
                v.tensor_scalar(out=b.sm[:, 56 + tc:57 + tc], in0=b.sm[:, 52 + tc:53 + tc],
                                scalar1=1.0 / D, scalar2=EPS, op0=ALU.mult, op1=ALU.add)
                v.tensor_tensor(out=b.sm[:, 60 + tc:61 + tc], in0=b.stats[:, 48 + 2 * tc:49 + 2 * tc],
                                in1=b.stats[:, 48 + 2 * tc:49 + 2 * tc], op=ALU.mult)
                v.tensor_tensor(out=b.sm[:, 56 + tc:57 + tc], in0=b.sm[:, 56 + tc:57 + tc],
                                in1=b.sm[:, 60 + tc:61 + tc], op=ALU.subtract)
                v.reciprocal(out=b.stats[:, 49 + 2 * tc:50 + 2 * tc],
                             in_=b.sm[:, 56 + tc:57 + tc])
            b.mark(f"{pp}f:fvar", b.s_dv)
            b.wtm(v, f"{pp}f:fsqrt")
            for tc in range(4):
                for hf in range(2):
                    idx = 2 * tc + hf
                    if idx >= 2:
                        b.wtm(v, f"{pp}f:st:{(idx - 1) // 2}:{(idx - 1) % 2}")
                    v.tensor_scalar(out=AP(a1t, 4096 * tc + 2048 * hf, [[16384, 128], [1, 2048]]),
                                    in0=AP(a2t, 4096 * tc + 2048 * hf, [[16384, 128], [1, 2048]]),
                                    scalar1=b.stats[:, 48 + 2 * tc:49 + 2 * tc],
                                    scalar2=b.stats[:, 49 + 2 * tc:50 + 2 * tc],
                                    op0=ALU.subtract, op1=ALU.mult)
                    v.tensor_tensor(out=b.a3[:, 2048 * hf:2048 * (hf + 1)],
                                    in0=AP(a1t, 4096 * tc + 2048 * hf, [[16384, 128], [1, 2048]]),
                                    in1=AP(x2t, 4096 * tc + 2048 * hf, [[16384, 128], [1, 2048]]),
                                    op=ALU.add)
                    b.mark(f"{pp}f:out:{tc}:{hf}", b.s_dv)

        @b.block.scalar
        def _(sc):
            sc = SelfSync(b, sc, b.s_ds, _WRAP_S)
            for tc in range(4):
                b.wtm(sc, f"{pp}o:x2:{tc}")
                sl = AP(x2t, 4096 * tc, [[16384, 128], [1, 4096]])
                dmp = b.masks[:, 0:4096]
                sc.activation(out=dmp, in_=sl, func=AF.Copy,
                              accum_out=b.sm[:, 40 + tc:41 + tc])
                sc.activation(out=dmp, in_=sl, func=AF.Square,
                              accum_out=b.sm[:, 44 + tc:45 + tc])
                b.mark(f"{pp}f:x3s:{tc}", b.s_ds)
            b.mark(f"{pp}f:x3sq", b.s_ds)
            for tc in range(4):
                b.wtm(sc, f"{pp}f:x3var:{tc}")
                sc.activation(out=b.stats[:, 41 + 2 * tc:42 + 2 * tc],
                              in_=b.stats[:, 41 + 2 * tc:42 + 2 * tc], func=AF.Sqrt)
                b.mark(f"{pp}f:x3sqrt:{tc}", b.s_ds)
            for tc in range(4):
                b.wtm(sc, f"{pp}f:d21:{tc}")
                sc.activation(out=b.sm[:, 20 + tc:21 + tc], in_=b.sm[:, 20 + tc:21 + tc],
                              func=AF.Sigmoid)
                b.mark(f"{pp}f:sig:{tc}", b.s_ds)
            for og in range(8):
                b.wtm(sc, f"{pp}f:fue:{og}")
                for tc in range(4):
                    sl = AP(a2t, 4096 * tc + 512 * og, [[16384, 128], [1, 512]])
                    sc.activation(out=AP(b.a3[:].tensor, 0, [[4096, 128], [1, 512]]),
                                  in_=sl, func=AF.Copy,
                                  accum_out=b.sm[:, 8 + 8 * tc + og:9 + 8 * tc + og])
                    sc.activation(out=AP(b.a3[:].tensor, 512, [[4096, 128], [1, 512]]),
                                  in_=sl, func=AF.Square,
                                  accum_out=b.stats[:, 8 * tc + og:8 * tc + og + 1])
            b.mark(f"{pp}f:fsq", b.s_ds)
            b.wtm(sc, f"{pp}f:fvar")
            for tc in range(4):
                sc.activation(out=b.stats[:, 49 + 2 * tc:50 + 2 * tc],
                              in_=b.stats[:, 49 + 2 * tc:50 + 2 * tc], func=AF.Sqrt)
            b.mark(f"{pp}f:fsqrt", b.s_ds)

        @b.block.tensor
        def _(te):
            for tc in range(4):
                b.wtm(te, f"{pp}f:dmat:{tc}")
                if tc == 0:
                    b.wtm(te, f"{pp}o:x2done")
                    b.wtm(te, "pro:wup")
                if tc >= 1:
                    b.wtm(te, f"{pp}f:logit:{tc - 1}")
                for ic in range(NIC):
                    ins = te.matmul(out=b.pb[4][:, 0:8],
                                    lhsT=AP(a1t, 512 * ic + 128 * tc, [[16384, 128], [1, 128]]),
                                    rhs=b.wr_bf[:, 8 * ic:8 * (ic + 1)],
                                    start=(ic == 0), stop=(ic == NIC - 1))
                b.inc(ins, b.s_mm)
                b.mark(f"{pp}f:mmL:{tc}", b.s_mm)
            for og in range(8):
                if og >= 1:
                    b.wtm(te, f"{pp}f:fue:{og - 1}")
                b.wtm(te, f"{pp}f:wbt:{og}")
                for tc in range(4):
                    te.matmul(out=b.pb[tc][:], lhsT=b.onesb[0:1, :],
                              rhs=b.probs[0:1, 512:1024],
                              start=True, stop=False)
                for ic in range(NIC):
                    b.wtm(te, f"{pp}f:wb:{og}:{ic}")
                    wb_ = b.wchunk_buf(og * NIC + ic, 6)
                    for tc in range(4):
                        ins = te.matmul(out=b.pb[tc][:],
                                        lhsT=AP(a1t, 512 * ic + 128 * tc, [[16384, 128], [1, 128]]),
                                        rhs=wb_,
                                        start=False, stop=False)
                    b.inc(ins, b.s_mm)
                    b.mark(f"{pp}f:mmfb:{og}:{ic}", b.s_mm)
                if og == 0:
                    for tc in range(4):
                        b.wtm(te, f"{pp}f:comb:{tc}")
                        if tc >= 2:
                            b.wtm(te, f"{pp}f:ctev:{tc - 2}")
                        psx = b.pf[tc % 2]
                        b.inc(te.transpose(out=psx[0:8, 0:128], in_=b.logitb[:, 8 * tc:8 * (tc + 1)],
                                           identity=b.ident[:]), b.s_mm)
                        b.mark(f"{pp}f:mmct:{tc}", b.s_mm)
                    b.wtm(te, f"{pp}f:combT")
                    b.inc(te.matmul(out=b.pb[5][:], lhsT=b.e8[:], rhs=b.combT[:],
                                    start=True, stop=True), b.s_mm)
                    b.mark(f"{pp}f:mmcb", b.s_mm)
                    b.wtm(te, f"{pp}f:wbt:0")
                    b.wtm(te, f"{pp}f:logit:3")
                    for ic in range(NIC):
                        ins = te.matmul(out=b.pb[4][:],
                                        lhsT=b.wsmall[:, 128 * ic:128 * (ic + 1)],
                                        rhs=AP(a1t, 512 * ic, [[16384, 128], [1, 512]]),
                                        start=(ic == 0), stop=(ic == NIC - 1))
                    b.inc(ins, b.s_mm)
                    b.mark(f"{pp}f:mmh", b.s_mm)

                b.wtm(te, f"{pp}f:hw")
                for tc in range(4):
                    ins = te.matmul(out=b.pb[tc][:], lhsT=b.hw[:, 128 * tc:128 * (tc + 1)],
                                    rhs=b.probs[:, 0:512],
                                    start=False, stop=True)
                b.inc(ins, b.s_mm)
                b.mark(f"{pp}f:mmf:{og}", b.s_mm)

    def emit_taps(self):
        b = self
        @b.block.gpsimd
        def _(g):
            b.wt(g, b.s_dv)
            b.wt(g, b.s_ds)
            b.wt(g, b.s_mm)
            def dump(name, sbt, base, ntile, ncol):
                hb = b.tap_h[name]
                for i in range(ntile):
                    b.inc(g.dma_start(out=hb[128 * i:128 * (i + 1), :],
                                      in_=AP(sbt[:].tensor, base + ncol * i,
                                             [[16384, 128], [1, ncol]])), b.s_st, 16)
            if "kcvcT" in b.taps:
                dump("kcvcT", b.a2, 0, 4, 2048)
            if "latT" in b.taps:
                dump("latT", b.a2, 8192, 4, 2048)
            if "KT" in b.taps:
                dump("KT", b.KT, 0, 8, 2048)
            if "V" in b.taps:
                dump("V", b.V, 0, 16, 1024)
            if "x1T" in b.taps:
                dump("x1T", b.a1, 0, 32, 512)
            if "QT" in b.taps:
                dump("QT", b.a2, 0, 32, 512)
            if "attn_outT" in b.taps:
                dump("attn_outT", b.a1, 0, 32, 512)
            if "B" in b.taps:
                b.inc(g.dma_start(out=b.tap_h["B"][:, 0:1], in_=b.bvec[:],
                                  allow_slow_non_contiguous=True), b.s_st, 16)
                b.inc(g.dma_start(out=b.tap_h["B"][0:1, 1:4], in_=b.sm[0:1, 56:59]), b.s_st, 16)
            if "x2" in b.taps:
                dump("x2", b.x2, 0, 4, 4096)
            if "x3T" in b.taps:
                dump("x3T", b.a1, 0, 32, 512)
            if "cb" in b.taps:
                b.inc(g.dma_start(out=b.tap_h["cb"][:], in_=b.cb[:]), b.s_st, 16)
            if "hw" in b.taps:
                b.inc(g.dma_start(out=b.tap_h["hw"][:], in_=b.hw[:]), b.s_st, 16)
            if "fU" in b.taps:
                dump("fU", b.a2, 0, 4, 4096)
            if "logit" in b.taps:
                b.inc(g.dma_start(out=b.tap_h["logit"][:], in_=b.logit[:]), b.s_st, 16)

    def emit_end(self):
        b = self
        @b.block.sync
        def _(sync):
            b.wt(sync, b.s_st)

    def emit_all(self):
        self.emit_consts()
        self.emit_prologue()
        if self.stop_after != "prologue":
            self.emit_pass(0)
            if self.stop_after == "full":
                self.emit_pass(1)
        if self.taps:
            self.emit_taps()
        self.emit_end()
        return self.finish()


def build_program(taps=(), stop_after="full"):
    rec = Build(taps=taps, stop_after=stop_after)
    rec.emit_all()
    return Build(taps=taps, stop_after=stop_after, marks=rec.marks).emit_all()


def make_masks(part):
    perm = PERMS[part]
    out = []
    for P, qb in enumerate(([0, 3] if part == 0 else [1, 2])):
        NC = NCP[P]
        m = np.zeros((128, 4096), np.float32)
        k = np.arange(128)[:, None]
        q = np.arange(512)[None, :]
        pa = np.asarray(perm)
        for mj in range(8):
            kc = NC - 8 + mj
            g = 2 * kc + k // 64
            ktok = 64 * (4 * (g // 4) + pa[g % 4]) + (k % 64)
            qtok = 512 * qb + q
            m[:, 512 * mj:512 * (mj + 1)] = (ktok <= qtok).astype(np.float32)
        out.append(m.astype(NPBF))
    return out


def _prep_shared(inputs):
    f32 = np.float32
    g = {}
    g["wq_t"] = np.ascontiguousarray(np.asarray(inputs["L0_attn_Wq"], f32).T)
    wkv = np.concatenate([np.asarray(inputs["L0_attn_Wk_down"], f32),
                          np.asarray(inputs["L0_attn_Wv_down"], f32)], axis=0)
    g["wkv_t"] = np.ascontiguousarray(wkv.T)
    g["wku_t"] = np.ascontiguousarray(np.asarray(inputs["L0_attn_Wk_up"], f32).T)
    g["wvu_t"] = np.ascontiguousarray(np.asarray(inputs["L0_attn_Wv_up"], f32).T)
    g["wo_t"] = np.ascontiguousarray(np.asarray(inputs["L0_attn_Wo"], f32).T)
    g["wb_t"] = np.ascontiguousarray(np.asarray(inputs["L0_ffn_W_base"], f32).T)
    g["at_t"] = np.ascontiguousarray(np.asarray(inputs["L0_ffn_A"], f32).reshape(128, D).T)
    g["bt_t"] = np.ascontiguousarray(
        np.asarray(inputs["L0_ffn_B"], f32).transpose(0, 2, 1).reshape(128, D))
    g["wr_t"] = np.ascontiguousarray(np.asarray(inputs["L0_ffn_W_router"], f32).T)
    g["bb"] = np.ascontiguousarray(np.asarray(inputs["L0_ffn_b_base"], f32))
    g["cst_ident"] = np.eye(128, dtype=NPBF)
    e8 = np.zeros((8, 128), np.float32)
    for e in range(8):
        e8[e, 16 * e:16 * (e + 1)] = 2.0
    g["cst_e8"] = e8.astype(NPBF)
    g["cst_ones_f32"] = np.ones((128, 128), np.float32)
    g["cst_ones_bf"] = np.ones((128, 128), NPBF)
    return g


def make_in_maps(inputs):
    x = np.asarray(inputs["L0_attn_hidden_states"], np.float32)
    g = _prep_shared(inputs)
    masks = [make_masks(0), make_masks(1)]
    in_maps = []
    for c in range(8):
        bb, p = c // 2, c % 2
        m = dict(g)
        perm = PERMS[p]
        m["x"] = np.ascontiguousarray(
            np.concatenate([x[bb, 512 * blk:512 * (blk + 1)] for blk in perm], axis=0))
        m["cst_masksA"] = masks[p][0]
        m["cst_masksB"] = masks[p][1]
        in_maps.append(m)
    return in_maps


def kernel(**inputs):
    in_maps = make_in_maps(inputs)
    nc = build_program()
    res = run_bass_kernel_spmd(nc, in_maps, core_ids=list(range(8)))
    out = np.empty((4, S, D), np.float32)
    for c in range(8):
        bb, p = c // 2, c % 2
        qbs = [0, 3] if p == 0 else [1, 2]
        r = res.results[c]["out"]
        out[bb, 512 * qbs[0]:512 * (qbs[0] + 1)] = r[0:512]
        out[bb, 512 * qbs[1]:512 * (qbs[1] + 1)] = r[512:1024]
    return out


# version 14
# speedup vs baseline: 1.0596x; 1.0596x over previous
import numpy as np
import ml_dtypes
import concourse.bass as bass
import concourse.mybir as mybir
from concourse.bass_utils import run_bass_kernel_spmd
from contextlib import ExitStack

F32 = mybir.dt.float32
BF16 = mybir.dt.bfloat16
AF = mybir.ActivationFunctionType
ALU = mybir.AluOpType
AX = mybir.AxisListType
AP = bass.AP
NPBF = ml_dtypes.bfloat16

D = 4096
S = 2048
HK, H, DH = 8, 32, 128
NIC = 32
EPS = 1e-5
ISQ = float(1.0 / np.sqrt(128.0))
NCP = [8, 16]
PERMS = [[0, 1, 3, 2], [1, 0, 2, 3]]


class Sem:
    def __init__(self, handle, name):
        self.h, self.n, self.name = handle, 0, name


_WRAP_V = {"tensor_copy", "tensor_tensor", "tensor_add", "tensor_sub", "tensor_mul",
           "tensor_max", "tensor_scalar", "tensor_scalar_add", "tensor_scalar_sub",
           "tensor_scalar_mul", "tensor_scalar_max", "tensor_scalar_min",
           "tensor_single_scalar", "reduce_sum", "reduce_max", "tensor_reduce",
           "reciprocal", "scalar_tensor_tensor", "memset", "select", "tensor_relu"}
_WRAP_S = {"activation", "copy", "add", "mul", "memzero"}


class SelfSync:

    def __init__(self, build, eng, sem, names):
        self._b, self._e, self._s, self._names = build, eng, sem, names

    def __getattr__(self, name):
        f = getattr(self._e, name)
        if name not in self._names:
            return f
        def wrap(*a, **k):
            s = self._s
            if s.n > 0:
                self._e.wait_ge(s.h, s.n)
                self._b.waits.append((s, s.n))
            ins = f(*a, **k)
            ins.then_inc(s.h, 1)
            s.n += 1
            return ins
        return wrap


class Build:
    def __init__(self, taps=(), stop_after="full", marks=None):
        self.taps = set(taps)
        self.stop_after = stop_after
        self.rec = marks is None
        self.marks = {} if marks is None else marks
        nc = bass.Bass("TRN2", target_bir_lowering=False)
        self.nc = nc
        self.stack = ExitStack()
        self.waits = []

        dt = lambda n, s, d=F32, k="ExternalInput": nc.dram_tensor(n, s, d, kind=k).ap()
        self.x_h = dt("x", [S, D])
        self.wq_h = dt("wq_t", [D, D])
        self.wkv_h = dt("wkv_t", [D, 512])
        self.wku_h = dt("wku_t", [256, 1024])
        self.wvu_h = dt("wvu_t", [256, 1024])
        self.wo_h = dt("wo_t", [D, D])
        self.wb_h = dt("wb_t", [D, D])
        self.at_h = dt("at_t", [D, 128])
        self.bt_h = dt("bt_t", [128, D])
        self.wr_h = dt("wr_t", [D, 8])
        self.bb_h = dt("bb", [D])
        self.ident_h = dt("cst_ident", [128, 128], BF16)
        self.masksA_h = dt("cst_masksA", [128, 4096], BF16)
        self.masksB_h = dt("cst_masksB", [128, 4096], BF16)
        self.e8_h = dt("cst_e8", [8, 128], BF16)
        self.onesf_h = dt("cst_ones_f32", [128, 128], F32)
        self.onesb_h = dt("cst_ones_bf", [128, 128], BF16)
        self.out_h = dt("out", [1024, D], F32, "ExternalOutput")
        self.tap_h = {}
        for t in self.taps:
            shp = {"kcvcT": [512, S], "latT": [512, S], "KT": [1024, S],
                   "V": [S, 1024], "QT": [4096, 512], "attn_outT": [4096, 512],
                   "x2": [512, D], "x1T": [4096, 512], "B": [128, 8],
                   "x3T": [4096, 512], "cb": [128, 512], "hw": [128, 512],
                   "fU": [512, D], "logit": [128, 48]}[t]
            self.tap_h[t] = dt("tap_" + t, shp, F32, "ExternalOutput")

        sb = lambda n, s, d: self.stack.enter_context(nc.sbuf_tensor(n, s, d))
        self.KT = sb("KT", [128, 16384], BF16)
        self.V = sb("V", [128, 16384], BF16)
        self.x2 = sb("x2", [128, 16384], BF16)
        self.a1 = sb("a1", [128, 16384], BF16)
        self.a2 = sb("a2", [128, 16384], BF16)
        self.a3 = sb("a3", [128, 4096], F32)
        self.wbuf = sb("wbuf", [128, 2048], BF16)
        self.wsmall = sb("wsmall", [128, 4096], BF16)
        self.ident = sb("ident", [128, 128], BF16)
        self.masks = sb("masks", [128, 4096], BF16)
        self.onesf = sb("onesf", [128, 128], F32)
        self.onesb = sb("onesb", [128, 128], BF16)
        self.e8 = sb("e8", [8, 128], BF16)
        self.wr_bf = sb("wr_bf", [128, 256], BF16)
        self.probs = sb("probs", [128, 2048], BF16)
        self.recb = sb("recb", [128, 512], F32)
        self.cb = sb("cb", [128, 512], BF16)
        self.hw = sb("hw", [128, 512], BF16)
        self.stats = sb("stats", [128, 96], F32)
        self.sm = sb("sm", [128, 64], F32)
        self.logit = sb("logit", [128, 48], F32)
        self.logitb = sb("logitb", [128, 32], BF16)
        self.combT = sb("combT", [8, 512], BF16)
        self.bvec = sb("bvec", [128, 1], F32)

        ps = lambda n, s, d: self.stack.enter_context(nc.psum_tensor(n, s, d))
        self.pb = [ps(f"pb{i}", [128, 512], F32) for i in range(6)]
        self.pf = [ps(f"pf{i}", [128, 1024], BF16) for i in range(2)]

        mk = lambda n: Sem(self.stack.enter_context(nc.semaphore(n)), n)
        self.s_x = mk("s_x")
        self.s_x1 = mk("s_x1")
        self.s_c = mk("s_c")
        self.s_t = mk("s_t")
        self.s_t1 = mk("s_t1")
        self.s_we = mk("s_we")
        self.s_wo = mk("s_wo")
        self.s_w2 = mk("s_w2")
        self.s_w3 = mk("s_w3")
        self.s_w4 = mk("s_w4")
        self.s_w5 = mk("s_w5")
        self.s_w6 = mk("s_w6")
        self.s_w7 = mk("s_w7")
        self.s_ws = mk("s_ws")
        self.s_mm = mk("s_mm")
        self.s_st = mk("s_st")
        self.s_dv = mk("s_dv")
        self.s_ds = mk("s_ds")
        self.semmap = {s.name: s for s in
                       [self.s_x, self.s_x1, self.s_c, self.s_t, self.s_t1, self.s_we, self.s_wo,
                        self.s_w2, self.s_w3, self.s_w4, self.s_w5, self.s_w6,
                        self.s_w7,
                        self.s_ws, self.s_mm, self.s_st, self.s_dv, self.s_ds]}
        self.block = self.stack.enter_context(nc.Block())

    def wsem(self, idx, depth=4):
        return [self.s_we, self.s_wo, self.s_w2, self.s_w3,
                self.s_w4, self.s_w5, self.s_w6, self.s_w7][idx % depth]

    def wchunk_buf(self, idx, depth=4):
        k = idx % depth
        if k < 4:
            return self.wbuf[:, 512 * k:512 * (k + 1)]
        if depth == 6:
            return self.probs[:, 1024 + 512 * (k - 4):1024 + 512 * (k - 3)]
        return self.wsmall[:, 512 * (k - 4):512 * (k - 3)]

    def mark(self, name, sem):
        if self.rec:
            assert name not in self.marks, f"dup mark {name}"
            self.marks[name] = (sem.name, sem.n)
        else:
            assert self.marks[name] == (sem.name, sem.n), \
                f"mark {name}: {self.marks[name]} != {(sem.name, sem.n)}"

    def wtm(self, eng, name):
        if self.rec:
            return
        semname, val = self.marks[name]
        sem = self.semmap[semname]
        self.waits.append((sem, val))
        eng.wait_ge(sem.h, val)

    def wt(self, eng, sem, val=None):
        if val is None:
            val = sem.n
        self.waits.append((sem, val))
        eng.wait_ge(sem.h, val)

    def inc(self, instr, sem, k=1):
        instr.then_inc(sem.h, k)
        sem.n += k
        return sem.n

    def finish(self):
        if not self.rec:
            for sem, val in self.waits:
                assert val <= sem.n, f"wait {sem.name}>={val} but total {sem.n}"
        self.stack.close()
        return self.nc

    def emit_consts(self):
        b = self
        @b.block.sync
        def _(sync):
            for dst, src in [(b.ident, b.ident_h), (b.onesf, b.onesf_h),
                             (b.onesb, b.onesb_h), (b.e8, b.e8_h)]:
                b.inc(sync.dma_start(out=dst[:], in_=src[:]), b.s_c, 16)
            b.mark("consts", b.s_c)
        @b.block.gpsimd
        def _(g):
            b.inc(g.dma_start(
                out=AP(b.wr_bf[:].tensor, 0, [[256, 128], [8, 32], [1, 8]]),
                in_=AP(b.wr_h.tensor, 0, [[8, 128], [1024, 32], [1, 8]])), b.s_ws, 16)
            b.mark("wr", b.s_ws)

    def emit_prologue(self):
        b = self
        XS = 8192
        SQ = 12288
        x2t = b.x2[:].tensor
        a1t = b.a1[:].tensor
        a2t = b.a2[:].tensor
        P1 = [(t_lo, Thi, g) for t_lo in range(8) for Thi in range(2) for g in range(4)]
        P2 = [(u, Thi, hHi, h4) for u in range(4) for Thi in range(2)
              for hHi in range(2) for h4 in range(4)]

        @b.block.sync
        def _(sync):
            ktt = b.KT[:].tensor
            def dmat(tt):
                b.wtm(sync, f"pro:x1w:{tt}")
                if tt % 4 == 0 and tt >= 8:
                    b.wtm(sync, f"pro:mmkv:{tt // 4 - 2}:{NIC - 1}")
                stg_ = XS if tt % 2 == 0 else 0
                st_ = b.s_t if tt % 2 == 0 else b.s_t1
                dstt = a1t if (tt // 4) % 2 == 0 else ktt
                b.inc(sync.dma_start_transpose(
                    out=AP(dstt, 128 * (tt % 4), [[16384, 128], [512, 32], [1, 128]]),
                    in_=AP(x2t, stg_, [[16384, 128], [128, 32], [1, 128]])),
                    st_, 16)
                b.mark(f"pro:dmat:{tt}", st_)
            for t in range(16):
                if t >= 1:
                    b.wtm(sync, f"pro:xfree:{t - 1}:0")
                    b.wtm(sync, f"pro:xfreed:{t - 1}:0")
                b.inc(sync.dma_start(
                    out=b.a3[:, 0:2048],
                    in_=b.x_h[128 * t:128 * (t + 1), 0:2048]),
                    b.s_x, 16)
                b.mark(f"pro:x:{t}:0", b.s_x)
                if t == 0:
                    b.inc(sync.dma_start(
                        out=b.a3[:, 2048:4096],
                        in_=b.x_h[0:128, 2048:4096]),
                        b.s_x1, 16)
                    b.mark("pro:x:0:1", b.s_x1)
                if t >= 1:
                    dmat(t - 1)
            dmat(15)

        @b.block.gpsimd
        def _(g):
            for srcw, base in [(b.wku_h, 0), (b.wvu_h, 2048)]:
                for jc in range(2):
                    b.inc(g.dma_start(out=b.wsmall[:, base + 1024 * jc:base + 1024 * (jc + 1)],
                                      in_=srcw[128 * jc:128 * (jc + 1), :]), b.s_ws, 16)
            b.mark("pro:wup", b.s_ws)
            for s in range(4):
                for ic in range(NIC):
                    idx = s * NIC + ic
                    if idx >= 4:
                        pg, pi = divmod(idx - 4, NIC)
                        b.wtm(g, f"pro:mmkv:{pg}:{pi}")
                    buf = (idx % 4) * 512
                    sem = b.wsem(idx)
                    b.inc(g.dma_start(out=b.wbuf[:, buf:buf + 512],
                                      in_=b.wkv_h[128 * ic:128 * (ic + 1), :]), sem, 16)
                    b.mark(f"pro:wkv:{s}:{ic}", sem)

        @b.block.scalar
        def _(sc):
            sc = SelfSync(b, sc, b.s_ds, _WRAP_S)
            def sq(t):
                sqc = 8 + 2 * t
                b.wtm(sc, f"pro:x:{t}:0")
                sc.activation(out=AP(x2t, SQ, [[16384, 128], [1, 2048]]),
                              in_=b.a3[:, 0:2048], func=AF.Square,
                              accum_out=b.sm[:, sqc:sqc + 1])
                b.mark(f"pro:xfree:{t}:0", b.s_ds)
                b.wtm(sc, f"pro:x:{t}:1")
                sc.activation(out=AP(x2t, SQ + 2048, [[16384, 128], [1, 2048]]),
                              in_=b.a3[:, 2048:4096], func=AF.Square,
                              accum_out=b.sm[:, sqc + 1:sqc + 2])
                b.mark(f"pro:xfree:{t}:1", b.s_ds)
                b.mark(f"pro:sq:{t}", b.s_ds)
                if t + 1 < 16:
                    b.wtm(sc, f"pro:xfreed:{t}:1")
                    b.wt(sc, b.s_ds)
                    b.inc(sc.dma_start(
                        out=b.a3[:, 2048:4096],
                        in_=b.x_h[128 * (t + 1):128 * (t + 2), 2048:4096]),
                        b.s_x1, 16)
                    b.mark(f"pro:x:{t + 1}:1", b.s_x1)
            def sqr(t):
                mc = 64 + 2 * t
                b.wtm(sc, f"pro:var:{t}")
                sc.activation(out=b.stats[:, mc + 1:mc + 2],
                              in_=b.stats[:, mc + 1:mc + 2], func=AF.Sqrt)
                b.mark(f"pro:sqrt:{t}", b.s_ds)
            sq(0)
            for t in range(1, 16):
                sqr(t - 1)
                sq(t)
            sqr(15)
            P1_ = [(t_lo, Thi, g) for t_lo in range(8) for Thi in range(2) for g in range(4)]
            P2_ = [(u, Thi, hHi, h4) for u in range(4) for Thi in range(2)
                   for hHi in range(2) for h4 in range(4)]
            for i, (t_lo, Thi, g) in enumerate(P1_):
                if i % 2 != 1:
                    continue
                b.wtm(sc, f"pro:scr1:{i}")
                psx = b.pf[i % 2]
                srcp = AP(psx[:].tensor, 0, [[1024, 128], [32, 4], [1, 32]])
                dstp = AP(x2t, 4096 * Thi + 512 * t_lo + 256 * (g // 2) + 4 * (g % 2),
                          [[16384, 128], [1, 4], [8, 32]])
                sc.copy(out=dstp, in_=srcp)
                b.mark(f"pro:scr1e:{i}", b.s_ds)
            for i, (u, Thi, hHi, h4) in enumerate(P2_):
                if (64 + i) % 2 != 1:
                    continue
                b.wtm(sc, f"pro:scr2:{i}")
                psx = b.pf[(64 + i) % 2]
                srcp = AP(psx[:].tensor, 0, [[1024, 128], [16, 8], [1, 16]])
                dstp = AP(a2t, 8192 + 2048 * u + 1024 * hHi + 256 * h4 + 128 * Thi,
                          [[16384, 128], [16, 8], [1, 16]])
                sc.copy(out=dstp, in_=srcp)
                b.mark(f"pro:scr2e:{i}", b.s_ds)
            for i in range(32):
                h2, rb = i // 4, i % 4
                b.wtm(sc, f"pro:kte:{h2}:{rb}")
                if i >= 2:
                    b.wtm(sc, f"pro:kn:{i - 2}")
                half = (i % 2) * 512
                sc.activation(out=AP(x2t, SQ + half, [[16384, 128], [1, 512]]),
                              in_=b.KT[:, 2048 * h2 + 512 * rb:2048 * h2 + 512 * (rb + 1)],
                              func=AF.Square)
                b.mark(f"pro:ksq:{i}", b.s_ds)

        @b.block.tensor
        def _(te):
            b.wtm(te, "consts")
            ktt2 = b.KT[:].tensor
            for s in range(4):
                b.wtm(te, f"pro:dmat:{4 * s + 2}")
                b.wtm(te, f"pro:dmat:{4 * s + 3}")
                if s >= 1:
                    b.wtm(te, f"pro:kcvce:{s - 1}")
                stript = a1t if s % 2 == 0 else ktt2
                for ic in range(NIC):
                    b.wtm(te, f"pro:wkv:{s}:{ic}")
                    buf = ((s * NIC + ic) % 4) * 512
                    for fc in range(4):
                        ins = te.matmul(out=b.pb[fc][:],
                                        lhsT=b.wbuf[:, buf + 128 * fc:buf + 128 * (fc + 1)],
                                        rhs=AP(stript, 512 * ic, [[16384, 128], [1, 512]]),
                                        start=(ic == 0), stop=(ic == NIC - 1))
                    b.inc(ins, b.s_mm)
                    b.mark(f"pro:mmkv:{s}:{ic}", b.s_mm)
            for i, (t_lo, Thi, g) in enumerate(P1):
                if i == 0:
                    b.wtm(te, "pro:kcvce:3")
                if i >= 2:
                    b.wtm(te, f"pro:scr1e:{i - 2}")
                psx = b.pf[i % 2]
                in_ap = AP(a2t, 2048 * g + 1024 * Thi + t_lo, [[16384, 128], [128, 8], [8, 16]])
                b.inc(te.transpose(out=psx[:, 0:128], in_=in_ap, identity=b.ident[:]), b.s_mm)
                b.mark(f"pro:scr1:{i}", b.s_mm)
            for i, (u, Thi, hHi, h4) in enumerate(P2):
                if i == 0:
                    b.wtm(te, "pro:scr1e:62")
                    b.wtm(te, "pro:scr1e:63")
                if i >= 2:
                    b.wtm(te, f"pro:scr2e:{i - 2}")
                psx = b.pf[(64 + i) % 2]
                in_ap = AP(x2t, 4096 * Thi + 1024 * u + 4 * hHi + h4,
                           [[16384, 128], [512, 2], [256, 2], [8, 32]])
                b.inc(te.transpose(out=psx[:, 0:128], in_=in_ap, identity=b.ident[:]), b.s_mm)
                b.mark(f"pro:scr2:{i}", b.s_mm)
            b.wtm(te, "pro:wup")
            for h2 in range(8):
                for rb in range(4):
                    i = 4 * h2 + rb
                    if i == 0:
                        b.wtm(te, "pro:scr2e:62")
                        b.wtm(te, "pro:scr2e:63")
                    if i >= 2:
                        b.wtm(te, f"pro:kte:{(i - 2) // 4}:{(i - 2) % 4}")
                    psU = b.pb[4 + i % 2]
                    for jc in range(2):
                        ins = te.matmul(out=psU[:],
                                        lhsT=b.wsmall[:, 1024 * jc + 128 * h2:1024 * jc + 128 * (h2 + 1)],
                                        rhs=AP(a2t, 8192 + 2048 * jc + 512 * rb, [[16384, 128], [1, 512]]),
                                        start=(jc == 0), stop=(jc == 1))
                    b.inc(ins, b.s_mm)
                    b.mark(f"pro:mmku:{h2}:{rb}", b.s_mm)
            for rc in range(16):
                for vg in range(2):
                    i = 2 * rc + vg
                    if i < 2:
                        b.wtm(te, f"pro:kte:7:{2 + i}")
                    else:
                        b.wtm(te, f"pro:ve:{(i - 2) // 2}:{(i - 2) % 2}")
                    psV = b.pb[4 + i % 2]
                    for jc in range(2):
                        ins = te.matmul(out=psV[:],
                                        lhsT=AP(a2t, 8192 + 2048 * (2 + jc) + 128 * rc,
                                                [[16384, 128], [1, 128]]),
                                        rhs=b.wsmall[:, 2048 + 1024 * jc + 512 * vg:2048 + 1024 * jc + 512 * (vg + 1)],
                                        start=(jc == 0), stop=(jc == 1))
                    b.inc(ins, b.s_mm)
                    b.mark(f"pro:mmv:{rc}:{vg}", b.s_mm)
            for i in range(32):
                b.wtm(te, f"pro:ksq:{i}")
                if i == 0:
                    b.wtm(te, "pro:ve:15:0")
                if i >= 1:
                    b.wtm(te, f"pro:kn:{i - 1}")
                half = (i % 2) * 512
                b.inc(te.matmul(out=b.pb[4][0:1, :], lhsT=b.onesb[:, 0:1],
                                rhs=AP(x2t, SQ + half, [[16384, 128], [1, 512]]),
                                start=True, stop=True), b.s_mm)
                b.mark(f"pro:mmkn:{i}", b.s_mm)

        @b.block.vector
        def _(v):
            v = SelfSync(b, v, b.s_dv, _WRAP_V)
            for t in range(16):
                sc0 = 40 + 2 * (t % 4)
                sqc = 8 + 2 * t
                mc = 64 + 2 * t
                stg = XS if t % 2 == 0 else 0
                b.wtm(v, f"pro:x:{t}:0")
                if t >= 2:
                    b.wtm(v, f"pro:dmat:{t - 2}")
                if t >= 4:
                    b.wtm(v, f"pro:var:{t - 4}")
                v.tensor_scalar(out=AP(x2t, stg, [[16384, 128], [1, 2048]]),
                                in0=b.a3[:, 0:2048],
                                scalar1=0.0, scalar2=0.0, op0=ALU.add,
                                op1=ALU.add,
                                accum_out=b.sm[:, sc0:sc0 + 1])
                b.mark(f"pro:xfreed:{t}:0", b.s_dv)
                b.wtm(v, f"pro:x:{t}:1")
                v.tensor_scalar(out=AP(x2t, stg + 2048, [[16384, 128], [1, 2048]]),
                                in0=b.a3[:, 2048:4096],
                                scalar1=0.0, scalar2=0.0, op0=ALU.add,
                                op1=ALU.add,
                                accum_out=b.sm[:, sc0 + 1:sc0 + 2])
                b.mark(f"pro:xfreed:{t}:1", b.s_dv)
                b.wtm(v, f"pro:sq:{t}")
                v.tensor_tensor(out=b.sm[:, 56:57], in0=b.sm[:, sc0:sc0 + 1],
                                in1=b.sm[:, sc0 + 1:sc0 + 2], op=ALU.add)
                v.tensor_scalar(out=b.stats[:, mc:mc + 1], in0=b.sm[:, 56:57],
                                scalar1=1.0 / D, scalar2=None, op0=ALU.mult)
                v.tensor_tensor(out=b.sm[:, 43:44], in0=b.sm[:, sqc:sqc + 1],
                                in1=b.sm[:, sqc + 1:sqc + 2], op=ALU.add)
                v.tensor_scalar(out=b.sm[:, 43:44], in0=b.sm[:, 43:44],
                                scalar1=1.0 / D, scalar2=EPS, op0=ALU.mult, op1=ALU.add)
                v.tensor_tensor(out=b.sm[:, 44:45], in0=b.stats[:, mc:mc + 1],
                                in1=b.stats[:, mc:mc + 1], op=ALU.mult)
                v.tensor_tensor(out=b.sm[:, 43:44], in0=b.sm[:, 43:44], in1=b.sm[:, 44:45],
                                op=ALU.subtract)
                v.reciprocal(out=b.stats[:, mc + 1:mc + 2], in_=b.sm[:, 43:44])
                b.mark(f"pro:var:{t}", b.s_dv)
                b.wtm(v, f"pro:sqrt:{t}")
                for hf in range(2):
                    v.tensor_scalar(out=AP(x2t, stg + 2048 * hf, [[16384, 128], [1, 2048]]),
                                    in0=AP(x2t, stg + 2048 * hf, [[16384, 128], [1, 2048]]),
                                    scalar1=b.stats[:, mc:mc + 1],
                                    scalar2=b.stats[:, mc + 1:mc + 2],
                                    op0=ALU.subtract, op1=ALU.mult)
                b.mark(f"pro:x1w:{t}", b.s_dv)
                evs = {7: [0], 11: [1], 15: [2, 3]}.get(t, [])
                for s in evs:
                    b.wtm(v, f"pro:mmkv:{s}:{NIC - 1}")
                    for fc in range(4):
                        v.tensor_copy(out=AP(a2t, 2048 * fc + 512 * s, [[16384, 128], [1, 512]]),
                                      in_=b.pb[fc][:])
                    b.mark(f"pro:kcvce:{s}", b.s_dv)
            for i, (t_lo, Thi, g) in enumerate(P1):
                if i % 2 != 0:
                    continue
                b.wtm(v, f"pro:scr1:{i}")
                psx = b.pf[i % 2]
                src = AP(psx[:].tensor, 0, [[1024, 128], [32, 4], [1, 32]])
                dst = AP(x2t, 4096 * Thi + 512 * t_lo + 256 * (g // 2) + 4 * (g % 2),
                         [[16384, 128], [1, 4], [8, 32]])
                v.tensor_copy(out=dst, in_=src)
                b.mark(f"pro:scr1e:{i}", b.s_dv)
            for i, (u, Thi, hHi, h4) in enumerate(P2):
                if (64 + i) % 2 != 0:
                    continue
                b.wtm(v, f"pro:scr2:{i}")
                psx = b.pf[(64 + i) % 2]
                src = AP(psx[:].tensor, 0, [[1024, 128], [16, 8], [1, 16]])
                dst = AP(a2t, 8192 + 2048 * u + 1024 * hHi + 256 * h4 + 128 * Thi,
                         [[16384, 128], [16, 8], [1, 16]])
                v.tensor_copy(out=dst, in_=src)
                b.mark(f"pro:scr2e:{i}", b.s_dv)
            for h2 in range(8):
                for rb in range(4):
                    b.wtm(v, f"pro:mmku:{h2}:{rb}")
                    if h2 == 0 and rb == 0:
                        b.wtm(v, f"pro:mmkv:3:{NIC - 1}")
                    psU = b.pb[4 + (4 * h2 + rb) % 2]
                    v.tensor_copy(out=b.KT[:, 2048 * h2 + 512 * rb:2048 * h2 + 512 * (rb + 1)],
                                  in_=psU[:])
                    b.mark(f"pro:kte:{h2}:{rb}", b.s_dv)
            for rc in range(16):
                for vg in range(2):
                    b.wtm(v, f"pro:mmv:{rc}:{vg}")
                    psV = b.pb[4 + (2 * rc + vg) % 2]
                    v.tensor_copy(out=b.V[:, 1024 * rc + 512 * vg:1024 * rc + 512 * (vg + 1)],
                                  in_=psV[:])
                    b.mark(f"pro:ve:{rc}:{vg}", b.s_dv)
            for i in range(32):
                b.wtm(v, f"pro:mmkn:{i}")
                v.reduce_max(out=b.sm[0:1, 59:60], in_=b.pb[4][0:1, :], axis=AX.X)
                if i == 0:
                    v.tensor_copy(out=b.sm[0:1, 57:58], in_=b.sm[0:1, 59:60])
                else:
                    v.tensor_tensor(out=b.sm[0:1, 57:58], in0=b.sm[0:1, 57:58],
                                    in1=b.sm[0:1, 59:60], op=ALU.max)
                b.mark(f"pro:kn:{i}", b.s_dv)

    def emit_pass(self, P):
        b = self
        NC = NCP[P]
        x2t = b.x2[:].tensor
        a1t = b.a1[:].tensor
        a2t = b.a2[:].tensor
        pp = f"P{P}"
        pass_dv0 = b.s_dv.n
        pass_mm0 = b.s_mm.n
        pass_st0 = b.s_st.n

        @b.block.sync
        def _(sync):
            b.wt(sync, b.s_dv, pass_dv0)
            if P >= 1:
                b.wt(sync, b.s_st, pass_st0)
            b.inc(sync.dma_start(out=b.masks[:],
                                 in_=(b.masksA_h if P == 0 else b.masksB_h)[:]), b.s_c, 16)
            b.mark(f"{pp}i:masks", b.s_c)
            def dmat(tt):
                b.wtm(sync, f"{pp}i:x1w:{tt}")
                st_ = b.s_t if tt % 2 == 0 else b.s_t1
                b.inc(sync.dma_start_transpose(
                    out=AP(a1t, 128 * tt, [[16384, 128], [512, 32], [1, 128]]),
                    in_=AP(a2t, 4096 * (tt % 2), [[16384, 128], [128, 32], [1, 128]])),
                    st_, 16)
                b.mark(f"{pp}i:dmat:{tt}", st_)
            for t in range(4):
                if t >= 1:
                    b.wtm(sync, f"{pp}i:xfreed:{t - 1}:0")
                b.inc(sync.dma_start(
                    out=b.a3[:, 0:2048],
                    in_=b.x_h[1024 * P + 128 * t:1024 * P + 128 * (t + 1), 0:2048]),
                    b.s_x, 16)
                b.mark(f"{pp}i:x:{t}:0", b.s_x)
                if t == 0:
                    b.inc(sync.dma_start(
                        out=b.a3[:, 2048:4096],
                        in_=b.x_h[1024 * P:1024 * P + 128, 2048:4096]),
                        b.s_x1, 16)
                    b.mark(f"{pp}i:x:0:1", b.s_x1)
                if t >= 1:
                    dmat(t - 1)
            dmat(3)

        @b.block.gpsimd
        def _(g):
            for gg in range(8):
                for ic in range(NIC):
                    idx = gg * NIC + ic
                    if idx < 8:
                        if P == 0:
                            b.wtm(g, ("pro:mmkv:3:%d" % (28 + idx)) if idx < 4
                                  else "pro:mmku:7:3")
                        else:
                            b.wtm(g, ("P0f:mmfb:7:%d" % (28 + idx)) if idx < 4
                                  else "P0f:mmh")
                    else:
                        pg, pi = divmod(idx - 8, NIC)
                        b.wtm(g, f"{pp}i:mmq:{pg}:{pi}")
                    sem = b.wsem(idx, 8)
                    b.inc(g.dma_start(out=b.wchunk_buf(idx, 8),
                                      in_=b.wq_h[128 * ic:128 * (ic + 1), 512 * gg:512 * (gg + 1)]),
                          sem, 16)
                    b.mark(f"{pp}i:wq:{gg}:{ic}", sem)

        @b.block.scalar
        def _(sc):
            sc = SelfSync(b, sc, b.s_ds, _WRAP_S)
            for t in range(1, 4):
                b.wtm(sc, f"{pp}i:xfreed:{t - 1}:1")
                b.wt(sc, b.s_ds)
                b.inc(sc.dma_start(
                    out=b.a3[:, 2048:4096],
                    in_=b.x_h[1024 * P + 128 * t:1024 * P + 128 * (t + 1), 2048:4096]),
                    b.s_x1, 16)
                b.mark(f"{pp}i:x:{t}:1", b.s_x1)
            for i in range(32):
                g = i // 4
                b.wtm(sc, f"{pp}i:qte:{g}")
                if i >= 2:
                    b.wtm(sc, f"{pp}i:qn:{i - 2}")
                half = (i % 2) * 512
                sc.activation(out=AP(x2t, half, [[16384, 128], [1, 512]]),
                              in_=AP(a2t, 512 * i, [[16384, 128], [1, 512]]),
                              func=AF.Square)
                b.mark(f"{pp}i:qsq:{i}", b.s_ds)
            b.wtm(sc, f"{pp}i:bsq")
            sc.activation(out=b.sm[0:1, 58:59], in_=b.sm[0:1, 58:59], func=AF.Sqrt)
            b.mark(f"{pp}i:bsqrt", b.s_ds)

        @b.block.vector
        def _(v):
            v = SelfSync(b, v, b.s_dv, _WRAP_V)
            for t in range(4):
                stg = 4096 * (t % 2)
                mc = 64 + 16 * P + 2 * t
                b.wtm(v, f"pro:sqrt:{8 * P + t}")
                b.wtm(v, f"{pp}i:x:{t}:0")
                if t >= 2:
                    b.wtm(v, f"{pp}i:dmat:{t - 2}")
                v.tensor_scalar(out=AP(a2t, stg, [[16384, 128], [1, 2048]]),
                                in0=b.a3[:, 0:2048],
                                scalar1=b.stats[:, mc:mc + 1],
                                scalar2=b.stats[:, mc + 1:mc + 2],
                                op0=ALU.subtract, op1=ALU.mult)
                b.mark(f"{pp}i:xfreed:{t}:0", b.s_dv)
                b.wtm(v, f"{pp}i:x:{t}:1")
                v.tensor_scalar(out=AP(a2t, stg + 2048, [[16384, 128], [1, 2048]]),
                                in0=b.a3[:, 2048:4096],
                                scalar1=b.stats[:, mc:mc + 1],
                                scalar2=b.stats[:, mc + 1:mc + 2],
                                op0=ALU.subtract, op1=ALU.mult)
                b.mark(f"{pp}i:xfreed:{t}:1", b.s_dv)
                b.mark(f"{pp}i:x1w:{t}", b.s_dv)
            for g in range(8):
                b.wtm(v, f"{pp}i:mmq:{g}:{NIC - 1}")
                for hcl in range(4):
                    hc = 4 * g + hcl
                    v.tensor_copy(out=AP(a2t, 512 * hc, [[16384, 128], [1, 512]]),
                                  in_=b.pb[hcl][:])
                b.mark(f"{pp}i:qte:{g}", b.s_dv)
            for i in range(32):
                b.wtm(v, f"{pp}i:mmqn:{i}")
                v.reduce_max(out=b.sm[0:1, 59:60], in_=b.pb[4][0:1, :], axis=AX.X)
                if i == 0:
                    v.tensor_copy(out=b.sm[0:1, 56:57], in_=b.sm[0:1, 59:60])
                else:
                    v.tensor_tensor(out=b.sm[0:1, 56:57], in0=b.sm[0:1, 56:57],
                                    in1=b.sm[0:1, 59:60], op=ALU.max)
                b.mark(f"{pp}i:qn:{i}", b.s_dv)
            v.tensor_tensor(out=b.sm[0:1, 58:59], in0=b.sm[0:1, 56:57],
                            in1=b.sm[0:1, 57:58], op=ALU.mult)
            b.mark(f"{pp}i:bsq", b.s_dv)
            b.wtm(v, f"{pp}i:bsqrt")
            v.tensor_scalar(out=b.sm[0:1, 58:59], in0=b.sm[0:1, 58:59],
                            scalar1=-ISQ, scalar2=None, op0=ALU.mult)
            v.tensor_copy(out=b.recb[0:1, 0:1], in_=b.sm[0:1, 58:59])
            b.mark(f"{pp}i:negB", b.s_dv)
            b.wtm(v, f"{pp}i:mmB")
            v.tensor_copy(out=b.bvec[:], in_=b.pb[5][:, 0:1])
            b.mark(f"{pp}i:B", b.s_dv)

        @b.block.tensor
        def _(te):
            b.wt(te, b.s_dv, pass_dv0)
            for g in range(8):
                if g == 0:
                    b.wtm(te, f"{pp}i:dmat:2")
                    b.wtm(te, f"{pp}i:dmat:3")
                if g >= 1:
                    b.wtm(te, f"{pp}i:qte:{g - 1}")
                for ic in range(NIC):
                    b.wtm(te, f"{pp}i:wq:{g}:{ic}")
                    wb_ = b.wchunk_buf(g * NIC + ic, 8)
                    for hcl in range(4):
                        ins = te.matmul(out=b.pb[hcl][:],
                                        lhsT=wb_[:, 128 * hcl:128 * (hcl + 1)],
                                        rhs=AP(a1t, 512 * ic, [[16384, 128], [1, 512]]),
                                        start=(ic == 0), stop=(ic == NIC - 1))
                    b.inc(ins, b.s_mm)
                    b.mark(f"{pp}i:mmq:{g}:{ic}", b.s_mm)
            for i in range(32):
                b.wtm(te, f"{pp}i:qsq:{i}")
                if i >= 1:
                    b.wtm(te, f"{pp}i:qn:{i - 1}")
                half = (i % 2) * 512
                b.inc(te.matmul(out=b.pb[4][0:1, :], lhsT=b.onesb[:, 0:1],
                                rhs=AP(x2t, half, [[16384, 128], [1, 512]]),
                                start=True, stop=True), b.s_mm)
                b.mark(f"{pp}i:mmqn:{i}", b.s_mm)
            b.wtm(te, f"{pp}i:negB")
            b.inc(te.matmul(out=b.pb[5][:, 0:1], lhsT=b.onesf[0:1, :],
                            rhs=b.recb[0:1, 0:1], start=True, stop=True), b.s_mm)
            b.mark(f"{pp}i:mmB", b.s_mm)

        if self.stop_after == "i":
            return

        chunks = []
        for u in range(H // 2):
            for kc in range(NC):
                for hp in range(2):
                    chunks.append((2 * u + hp, kc))
        jlast = {}
        for j, (h, kc) in enumerate(chunks):
            if kc == NC - 1:
                jlast[h] = j
        slot_of_chunk = {}
        slot_of_rb = {}
        SB = [0, 1]

        @b.block.tensor
        def _(te):
            prev_user = {0: None, 1: None}
            slot_ctr2 = [0]

            def emit_scores(j):
                h, kc = chunks[j]
                h2 = h // 4
                if j == 0:
                    b.wtm(te, f"{pp}i:qte:7")
                sl = slot_ctr2[0] % 2
                slot_ctr2[0] += 1
                slot_of_chunk[j] = sl
                if prev_user[sl] is not None:
                    b.wtm(te, prev_user[sl])
                prev_user[sl] = f"{pp}ii:exp:{j}"
                ins = te.matmul(out=b.pb[SB[sl]][:],
                                lhsT=b.KT[:, 2048 * h2 + 128 * kc:2048 * h2 + 128 * (kc + 1)],
                                rhs=AP(a2t, 512 * h, [[16384, 128], [1, 512]]),
                                start=True, stop=True)
                b.inc(ins, b.s_mm)
                b.mark(f"{pp}ii:mmsc:{j}", b.s_mm)

            pending_rb = []

            def flush_rb(j):
                while pending_rb and pending_rb[0][1] <= j:
                    hh, _ = pending_rb.pop(0)
                    b.wtm(te, f"{pp}ii:recip:{hh}")
                    if hh == 0:
                        b.wtm(te, f"{pp}i:B")
                    else:
                        b.wtm(te, f"{pp}ii:norm:{hh - 1}")
                    r0 = 32 * (hh % 3)
                    b.inc(te.matmul(out=b.pb[5][:], lhsT=b.onesf[r0:r0 + 1, :],
                                    rhs=b.recb[r0:r0 + 1, :],
                                    start=True, stop=True), b.s_mm)
                    b.mark(f"{pp}ii:mmrb:{hh}", b.s_mm)

            def emit_tail(j):
                h, kc = chunks[j]
                h2 = h // 4
                flush_rb(j)
                if kc >= NC - 8:
                    b.wtm(te, f"{pp}ii:mask:{j}")
                else:
                    b.wtm(te, f"{pp}ii:exp:{j}")
                if kc == 0:
                    if h >= 2:
                        b.wtm(te, f"{pp}ii:evace:{h - 2}")
                    if h >= 3:
                        b.wtm(te, f"{pp}ii:recip:{h - 3}")
                    if h == 0:
                        b.wtm(te, f"{pp}i:qn:31")
                te.matmul(out=b.pb[2 + h % 2][:],
                          lhsT=b.V[:, 1024 * kc + 128 * h2:1024 * kc + 128 * (h2 + 1)],
                          rhs=b.probs[:, 512 * (j % 4):512 * (j % 4 + 1)],
                          start=(kc == 0), stop=(kc == NC - 1))
                ins = te.matmul(out=b.pb[4][32 * (h % 3):32 * (h % 3) + 1, :],
                                lhsT=b.onesb[:, 0:1],
                                rhs=b.probs[:, 512 * (j % 4):512 * (j % 4 + 1)],
                                start=(kc == 0), stop=(kc == NC - 1))
                b.inc(ins, b.s_mm)
                b.mark(f"{pp}ii:mmsu:{j}", b.s_mm)
                if kc == NC - 1:
                    pending_rb.append((h, j + 4 + 3 * (h % 2)))

            NCH = len(chunks)
            for j in range(NCH + 2):
                if j < NCH:
                    emit_scores(j)
                if j >= 2:
                    emit_tail(j - 2)
            flush_rb(10 ** 9)

        @b.block.scalar
        def _(sc):
            sc = SelfSync(b, sc, b.s_ds, _WRAP_S)
            for j, (h, kc) in enumerate(chunks):
                if j == 0:
                    b.wtm(sc, f"{pp}i:B")
                b.wtm(sc, f"{pp}ii:mmsc:{j}")
                if j >= 4:
                    b.wtm(sc, f"{pp}ii:mmsu:{j - 4}")
                sc.activation(out=b.probs[:, 512 * (j % 4):512 * (j % 4 + 1)],
                              in_=b.pb[SB[slot_of_chunk[j]]][:], func=AF.Exp,
                              bias=b.bvec[:], scale=ISQ)
                b.mark(f"{pp}ii:exp:{j}", b.s_ds)

        @b.block.vector
        def _(v):
            v = SelfSync(b, v, b.s_dv, _WRAP_V)
            first_mask = [True]
            pend_norm = []

            def flush_norm(j):
                while pend_norm and pend_norm[0][1] <= j:
                    hh, _ = pend_norm.pop(0)
                    b.wtm(v, f"{pp}ii:mmrb:{hh}")
                    v.tensor_tensor(out=AP(a1t, 512 * hh, [[16384, 128], [1, 512]]),
                                    in0=AP(a1t, 512 * hh, [[16384, 128], [1, 512]]),
                                    in1=b.pb[5][:], op=ALU.mult)
                    b.mark(f"{pp}ii:norm:{hh}", b.s_dv)

            for j, (h, kc) in enumerate(chunks):
                if kc >= NC - 8:
                    if first_mask[0]:
                        b.wtm(v, f"{pp}i:masks")
                        first_mask[0] = False
                    dj = kc - (NC - 8)
                    b.wtm(v, f"{pp}ii:exp:{j}")
                    v.tensor_tensor(out=b.probs[:, 512 * (j % 4):512 * (j % 4 + 1)],
                                    in0=b.probs[:, 512 * (j % 4):512 * (j % 4 + 1)],
                                    in1=b.masks[:, 512 * dj:512 * (dj + 1)],
                                    op=ALU.mult)
                    b.mark(f"{pp}ii:mask:{j}", b.s_dv)
                if kc == NC - 1:
                    b.wtm(v, f"{pp}ii:mmsu:{j}")
                    v.tensor_copy(out=AP(a1t, 512 * h, [[16384, 128], [1, 512]]),
                                  in_=b.pb[2 + h % 2][:])
                    b.mark(f"{pp}ii:evace:{h}", b.s_dv)
                    if h % 2 == 1:
                        for hh in (h - 1, h):
                            v.reciprocal(out=b.recb[32 * (hh % 3):32 * (hh % 3) + 1, :],
                                         in_=b.pb[4][32 * (hh % 3):32 * (hh % 3) + 1, :])
                            b.mark(f"{pp}ii:recip:{hh}", b.s_dv)
                    pend_norm.append((h, j + 4 + 3 * (h % 2)))
                flush_norm(j)
            flush_norm(10 ** 9)

        if self.stop_after == "ii":
            return

        mm_ii_last = b.s_mm.n

        @b.block.gpsimd
        def _(g):
            for og in range(8):
                for hc in range(NIC):
                    idx = og * NIC + hc
                    if idx < 8:
                        b.wtm(g, f"{pp}i:mmq:7:{24 + idx}")
                    else:
                        pg, pi = divmod(idx - 8, NIC)
                        b.wtm(g, f"{pp}o:mmo:{pg}:{pi}")
                    sem = b.wsem(idx, 8)
                    b.inc(g.dma_start(out=b.wchunk_buf(idx, 8),
                                      in_=b.wo_h[128 * hc:128 * (hc + 1), 512 * og:512 * (og + 1)]),
                          sem, 16)
                    b.mark(f"{pp}o:wo:{og}:{hc}", sem)

        @b.block.tensor
        def _(te):
            for og in range(8):
                if og >= 1:
                    b.wtm(te, f"{pp}o:aue:{og - 1}")
                for hc in range(NIC):
                    b.wtm(te, f"{pp}o:wo:{og}:{hc}")
                    if og == 0 and hc == 0:
                        b.wtm(te, f"{pp}ii:norm:{H - 1}")
                    wb_ = b.wchunk_buf(og * NIC + hc, 8)
                    for qc in range(4):
                        ins = te.matmul(out=b.pb[qc][:],
                                        lhsT=AP(a1t, 512 * hc + 128 * qc, [[16384, 128], [1, 128]]),
                                        rhs=wb_,
                                        start=(hc == 0), stop=(hc == NIC - 1))
                    b.inc(ins, b.s_mm)
                    b.mark(f"{pp}o:mmo:{og}:{hc}", b.s_mm)

        @b.block.vector
        def _(v):
            v = SelfSync(b, v, b.s_dv, _WRAP_V)
            for og in range(8):
                b.wtm(v, f"{pp}o:mmo:{og}:{NIC - 1}")
                for qc in range(4):
                    v.tensor_copy(out=AP(a2t, 4096 * qc + 512 * og, [[16384, 128], [1, 512]]),
                                  in_=b.pb[qc][:])
                b.mark(f"{pp}o:aue:{og}", b.s_dv)
            for qc in range(4):
                b.wtm(v, f"{pp}o:ausq:{qc}")
                v.reduce_sum(out=b.sm[:, 48 + qc:49 + qc],
                             in_=b.sm[:, 8 + 8 * qc:16 + 8 * qc], axis=AX.X)
                v.reduce_sum(out=b.sm[:, 52 + qc:53 + qc],
                             in_=b.stats[:, 8 * qc:8 * qc + 8], axis=AX.X)
                v.tensor_scalar(out=b.stats[:, 32 + 2 * qc:33 + 2 * qc],
                                in0=b.sm[:, 48 + qc:49 + qc],
                                scalar1=1.0 / D, scalar2=None, op0=ALU.mult)
                v.tensor_scalar(out=b.sm[:, 16 + qc:17 + qc], in0=b.sm[:, 52 + qc:53 + qc],
                                scalar1=1.0 / D, scalar2=EPS, op0=ALU.mult, op1=ALU.add)
                v.tensor_tensor(out=b.sm[:, 20 + qc:21 + qc], in0=b.stats[:, 32 + 2 * qc:33 + 2 * qc],
                                in1=b.stats[:, 32 + 2 * qc:33 + 2 * qc], op=ALU.mult)
                v.tensor_tensor(out=b.sm[:, 16 + qc:17 + qc], in0=b.sm[:, 16 + qc:17 + qc],
                                in1=b.sm[:, 20 + qc:21 + qc], op=ALU.subtract)
                v.reciprocal(out=b.stats[:, 33 + 2 * qc:34 + 2 * qc],
                             in_=b.sm[:, 16 + qc:17 + qc])
                b.mark(f"{pp}o:var:{qc}", b.s_dv)
                b.wtm(v, f"{pp}o:sqrt:{qc}")
                b.wtm(v, f"{pp}o:x:{qc}")
                for hf in range(2):
                    v.tensor_scalar(out=AP(a1t, 4096 * qc + 2048 * hf, [[16384, 128], [1, 2048]]),
                                    in0=AP(a2t, 4096 * qc + 2048 * hf, [[16384, 128], [1, 2048]]),
                                    scalar1=b.stats[:, 32 + 2 * qc:33 + 2 * qc],
                                    scalar2=b.stats[:, 33 + 2 * qc:34 + 2 * qc],
                                    op0=ALU.subtract, op1=ALU.mult)
                    v.tensor_tensor(out=AP(x2t, 4096 * qc + 2048 * hf, [[16384, 128], [1, 2048]]),
                                    in0=AP(a1t, 4096 * qc + 2048 * hf, [[16384, 128], [1, 2048]]),
                                    in1=b.a3[:, 2048 * hf:2048 * (hf + 1)], op=ALU.add)
                b.mark(f"{pp}o:x2:{qc}", b.s_dv)
            b.mark(f"{pp}o:x2done", b.s_dv)

        @b.block.scalar
        def _(sc):
            sc = SelfSync(b, sc, b.s_ds, _WRAP_S)
            for og in range(8):
                b.wtm(sc, f"{pp}o:aue:{og}")
                for qc in range(4):
                    sl = AP(a2t, 4096 * qc + 512 * og, [[16384, 128], [1, 512]])
                    sc.activation(out=AP(x2t, 0, [[16384, 128], [1, 512]]),
                                  in_=sl, func=AF.Copy,
                                  accum_out=b.sm[:, 8 + 8 * qc + og:9 + 8 * qc + og])
                    sc.activation(out=AP(x2t, 512, [[16384, 128], [1, 512]]),
                                  in_=sl, func=AF.Square,
                                  accum_out=b.stats[:, 8 * qc + og:8 * qc + og + 1])
                    if og == 7:
                        b.mark(f"{pp}o:ausq:{qc}", b.s_ds)
            b.mark(f"{pp}o:ausq", b.s_ds)
            for qc in range(4):
                b.wtm(sc, f"{pp}o:var:{qc}")
                sc.activation(out=b.stats[:, 33 + 2 * qc:34 + 2 * qc],
                              in_=b.stats[:, 33 + 2 * qc:34 + 2 * qc], func=AF.Sqrt)
                b.mark(f"{pp}o:sqrt:{qc}", b.s_ds)

        @b.block.sync
        def _(sync):
            for qc in range(4):
                for hf in range(2):
                    idx = 2 * qc + hf
                    if idx >= 2 and hf == 0:
                        b.wtm(sync, f"{pp}o:x2:{qc - 1}")
                    b.inc(sync.dma_start(
                        out=b.a3[:, 2048 * hf:2048 * (hf + 1)],
                        in_=b.x_h[1024 * P + 128 * qc:1024 * P + 128 * (qc + 1),
                                  2048 * hf:2048 * (hf + 1)]),
                        b.s_x, 16)
                b.mark(f"{pp}o:x:{qc}", b.s_x)

        if self.stop_after == "iii":
            return

        self.emit_ffn(P)

    def emit_ffn(self, P):
        b = self
        x2t = b.x2[:].tensor
        a1t = b.a1[:].tensor
        a2t = b.a2[:].tensor
        pp = f"P{P}"
        mm_iii_last = b.s_mm.n

        @b.block.sync
        def _(sync):
            for tc in range(4):
                b.wtm(sync, f"{pp}f:x3w:{tc}")
                b.inc(sync.dma_start_transpose(
                    out=AP(a1t, 128 * tc, [[16384, 128], [512, 32], [1, 128]]),
                    in_=AP(a2t, 0, [[16384, 128], [128, 32], [1, 128]])),
                    b.s_t, 16)
                b.mark(f"{pp}f:dmat:{tc}", b.s_t)
            for tc in range(4):
                for hf in range(2):
                    b.wtm(sync, f"{pp}f:out:{tc}:{hf}")
                    b.inc(sync.dma_start(
                        out=b.out_h[512 * P + 128 * tc:512 * P + 128 * (tc + 1),
                                    2048 * hf:2048 * (hf + 1)],
                        in_=b.a3[:, 2048 * hf:2048 * (hf + 1)]),
                        b.s_st, 16)
                    b.mark(f"{pp}f:st:{tc}:{hf}", b.s_st)

        @b.block.gpsimd
        def _(g):
            b.wtm(g, f"{pp}o:mmo:7:{NIC - 1}")
            b.inc(g.dma_start(out=AP(b.wsmall[:].tensor, 0, [[4096, 128], [128, 32], [1, 128]]),
                              in_=AP(b.at_h.tensor, 0, [[128, 128], [16384, 32], [1, 128]])),
                  b.s_ws, 16)
            b.mark(f"{pp}f:wat", b.s_ws)
            NCH_ = H * NCP[P]
            for og in range(8):
                if og == 0:
                    b.wtm(g, f"{pp}ii:mmsu:{NCH_ - 1}")
                if og >= 1:
                    b.wtm(g, f"{pp}f:mmf:{og - 1}")
                b.inc(g.dma_start(out=b.probs[:, 0:512],
                                  in_=b.bt_h[:, 512 * og:512 * (og + 1)]), b.s_ws, 16)
                b.inc(g.dma_start(out=b.probs[0:1, 512:1024],
                                  in_=b.bb_h[None, 512 * og:512 * (og + 1)]), b.s_ws, 16)
                b.mark(f"{pp}f:wbt:{og}", b.s_ws)
                for ic in range(NIC):
                    idx = og * NIC + ic
                    if idx < 6:
                        b.wtm(g, f"{pp}o:mmo:7:{28 + idx if idx < 4 else 24 + idx}")
                    else:
                        pg, pi = divmod(idx - 6, NIC)
                        b.wtm(g, f"{pp}f:mmfb:{pg}:{pi}")
                    sem = b.wsem(idx, 6)
                    b.inc(g.dma_start(out=b.wchunk_buf(idx, 6),
                                      in_=b.wb_h[128 * ic:128 * (ic + 1), 512 * og:512 * (og + 1)]),
                          sem, 16)
                    b.mark(f"{pp}f:wb:{og}:{ic}", sem)

        @b.block.vector
        def _(v):
            v = SelfSync(b, v, b.s_dv, _WRAP_V)
            for tc in range(4):
                b.wtm(v, f"{pp}f:x3s:{tc}")
                v.tensor_scalar(out=b.stats[:, 40 + 2 * tc:41 + 2 * tc],
                                in0=b.sm[:, 40 + tc:41 + tc],
                                scalar1=1.0 / D, scalar2=None, op0=ALU.mult)
                v.tensor_scalar(out=b.sm[:, 28 + tc:29 + tc], in0=b.sm[:, 44 + tc:45 + tc],
                                scalar1=1.0 / D, scalar2=EPS, op0=ALU.mult, op1=ALU.add)
                v.tensor_tensor(out=b.sm[:, 36 + tc:37 + tc], in0=b.stats[:, 40 + 2 * tc:41 + 2 * tc],
                                in1=b.stats[:, 40 + 2 * tc:41 + 2 * tc], op=ALU.mult)
                v.tensor_tensor(out=b.sm[:, 28 + tc:29 + tc], in0=b.sm[:, 28 + tc:29 + tc],
                                in1=b.sm[:, 36 + tc:37 + tc], op=ALU.subtract)
                v.reciprocal(out=b.stats[:, 41 + 2 * tc:42 + 2 * tc],
                             in_=b.sm[:, 28 + tc:29 + tc])
                b.mark(f"{pp}f:x3var:{tc}", b.s_dv)
                b.wtm(v, f"{pp}f:x3sqrt:{tc}")
                if tc >= 1:
                    b.wtm(v, f"{pp}f:dmat:{tc - 1}")
                v.tensor_scalar(out=AP(a2t, 0, [[16384, 128], [1, 4096]]),
                                in0=AP(x2t, 4096 * tc, [[16384, 128], [1, 4096]]),
                                scalar1=b.stats[:, 40 + 2 * tc:41 + 2 * tc],
                                scalar2=b.stats[:, 41 + 2 * tc:42 + 2 * tc],
                                op0=ALU.subtract, op1=ALU.mult)
                b.mark(f"{pp}f:x3w:{tc}", b.s_dv)
            b.mark(f"{pp}f:x3var", b.s_dv)
            for tc in range(4):
                b.wtm(v, f"{pp}f:mmL:{tc}")
                v.tensor_copy(out=b.logit[:, 8 * tc:8 * (tc + 1)], in_=b.pb[4][:, 0:8])
                b.mark(f"{pp}f:logit:{tc}", b.s_dv)
            for tc in range(4):
                lg = b.logit[:, 8 * tc:8 * (tc + 1)]
                v.reduce_max(out=b.sm[:, 8 + tc:9 + tc], in_=lg, axis=AX.X)
                v.tensor_scalar(out=b.logit[:, 32:40], in0=lg,
                                scalar1=b.sm[:, 8 + tc:9 + tc], scalar2=None, op0=ALU.is_equal)
                v.scalar_tensor_tensor(out=b.logit[:, 40:48], in0=b.logit[:, 32:40],
                                       scalar=-1e9, in1=lg, op0=ALU.mult, op1=ALU.add)
                v.reduce_max(out=b.sm[:, 12 + tc:13 + tc], in_=b.logit[:, 40:48], axis=AX.X)
                v.tensor_tensor(out=b.sm[:, 20 + tc:21 + tc], in0=b.sm[:, 8 + tc:9 + tc],
                                in1=b.sm[:, 12 + tc:13 + tc], op=ALU.subtract)
                b.mark(f"{pp}f:d21:{tc}", b.s_dv)
                b.wtm(v, f"{pp}f:sig:{tc}")
                v.tensor_scalar(out=b.logit[:, 40:48], in0=b.logit[:, 40:48],
                                scalar1=b.sm[:, 12 + tc:13 + tc], scalar2=None, op0=ALU.is_equal)
                v.tensor_scalar(out=b.sm[:, 16 + tc:17 + tc], in0=b.sm[:, 20 + tc:21 + tc],
                                scalar1=-1.0, scalar2=1.0, op0=ALU.mult, op1=ALU.add)
                v.tensor_scalar(out=b.logit[:, 32:40], in0=b.logit[:, 32:40],
                                scalar1=b.sm[:, 20 + tc:21 + tc], scalar2=None, op0=ALU.mult)
                v.tensor_scalar(out=b.logit[:, 40:48], in0=b.logit[:, 40:48],
                                scalar1=b.sm[:, 16 + tc:17 + tc], scalar2=None, op0=ALU.mult)
                v.tensor_tensor(out=b.logit[:, 32:40], in0=b.logit[:, 32:40],
                                in1=b.logit[:, 40:48], op=ALU.add)
                v.tensor_copy(out=b.logitb[:, 8 * tc:8 * (tc + 1)], in_=b.logit[:, 32:40])
                b.mark(f"{pp}f:comb:{tc}", b.s_dv)
            for tc in range(4):
                b.wtm(v, f"{pp}f:mmct:{tc}")
                psx = b.pf[tc % 2]
                v.tensor_copy(out=b.combT[0:8, 128 * tc:128 * (tc + 1)],
                              in_=psx[0:8, 0:128])
                b.mark(f"{pp}f:ctev:{tc}", b.s_dv)
            b.mark(f"{pp}f:combT", b.s_dv)
            b.wtm(v, f"{pp}f:mmcb")
            v.tensor_copy(out=b.cb[:], in_=b.pb[5][:])
            b.mark(f"{pp}f:cb", b.s_dv)
            b.wtm(v, f"{pp}f:mmh")
            v.tensor_tensor(out=b.hw[:], in0=b.pb[4][:], in1=b.cb[:], op=ALU.mult)
            b.mark(f"{pp}f:hw", b.s_dv)
            for og in range(8):
                b.wtm(v, f"{pp}f:mmf:{og}")
                for tc in range(4):
                    v.tensor_copy(out=AP(a2t, 4096 * tc + 512 * og, [[16384, 128], [1, 512]]),
                                  in_=b.pb[tc][:])
                b.mark(f"{pp}f:fue:{og}", b.s_dv)
            b.wtm(v, f"{pp}f:fsq")
            for tc in range(4):
                v.reduce_sum(out=b.sm[:, 48 + tc:49 + tc],
                             in_=b.sm[:, 8 + 8 * tc:16 + 8 * tc], axis=AX.X)
                v.reduce_sum(out=b.sm[:, 52 + tc:53 + tc],
                             in_=b.stats[:, 8 * tc:8 * tc + 8], axis=AX.X)
            for tc in range(4):
                v.tensor_scalar(out=b.stats[:, 48 + 2 * tc:49 + 2 * tc],
                                in0=b.sm[:, 48 + tc:49 + tc],
                                scalar1=1.0 / D, scalar2=None, op0=ALU.mult)
                v.tensor_scalar(out=b.sm[:, 56 + tc:57 + tc], in0=b.sm[:, 52 + tc:53 + tc],
                                scalar1=1.0 / D, scalar2=EPS, op0=ALU.mult, op1=ALU.add)
                v.tensor_tensor(out=b.sm[:, 60 + tc:61 + tc], in0=b.stats[:, 48 + 2 * tc:49 + 2 * tc],
                                in1=b.stats[:, 48 + 2 * tc:49 + 2 * tc], op=ALU.mult)
                v.tensor_tensor(out=b.sm[:, 56 + tc:57 + tc], in0=b.sm[:, 56 + tc:57 + tc],
                                in1=b.sm[:, 60 + tc:61 + tc], op=ALU.subtract)
                v.reciprocal(out=b.stats[:, 49 + 2 * tc:50 + 2 * tc],
                             in_=b.sm[:, 56 + tc:57 + tc])
            b.mark(f"{pp}f:fvar", b.s_dv)
            b.wtm(v, f"{pp}f:fsqrt")
            for tc in range(4):
                for hf in range(2):
                    idx = 2 * tc + hf
                    if idx >= 2:
                        b.wtm(v, f"{pp}f:st:{(idx - 1) // 2}:{(idx - 1) % 2}")
                    v.tensor_scalar(out=AP(a1t, 4096 * tc + 2048 * hf, [[16384, 128], [1, 2048]]),
                                    in0=AP(a2t, 4096 * tc + 2048 * hf, [[16384, 128], [1, 2048]]),
                                    scalar1=b.stats[:, 48 + 2 * tc:49 + 2 * tc],
                                    scalar2=b.stats[:, 49 + 2 * tc:50 + 2 * tc],
                                    op0=ALU.subtract, op1=ALU.mult)
                    v.tensor_tensor(out=b.a3[:, 2048 * hf:2048 * (hf + 1)],
                                    in0=AP(a1t, 4096 * tc + 2048 * hf, [[16384, 128], [1, 2048]]),
                                    in1=AP(x2t, 4096 * tc + 2048 * hf, [[16384, 128], [1, 2048]]),
                                    op=ALU.add)
                    b.mark(f"{pp}f:out:{tc}:{hf}", b.s_dv)

        @b.block.scalar
        def _(sc):
            sc = SelfSync(b, sc, b.s_ds, _WRAP_S)
            for tc in range(4):
                b.wtm(sc, f"{pp}o:x2:{tc}")
                sl = AP(x2t, 4096 * tc, [[16384, 128], [1, 4096]])
                dmp = b.masks[:, 0:4096]
                sc.activation(out=dmp, in_=sl, func=AF.Copy,
                              accum_out=b.sm[:, 40 + tc:41 + tc])
                sc.activation(out=dmp, in_=sl, func=AF.Square,
                              accum_out=b.sm[:, 44 + tc:45 + tc])
                b.mark(f"{pp}f:x3s:{tc}", b.s_ds)
            b.mark(f"{pp}f:x3sq", b.s_ds)
            for tc in range(4):
                b.wtm(sc, f"{pp}f:x3var:{tc}")
                sc.activation(out=b.stats[:, 41 + 2 * tc:42 + 2 * tc],
                              in_=b.stats[:, 41 + 2 * tc:42 + 2 * tc], func=AF.Sqrt)
                b.mark(f"{pp}f:x3sqrt:{tc}", b.s_ds)
            for tc in range(4):
                b.wtm(sc, f"{pp}f:d21:{tc}")
                sc.activation(out=b.sm[:, 20 + tc:21 + tc], in_=b.sm[:, 20 + tc:21 + tc],
                              func=AF.Sigmoid)
                b.mark(f"{pp}f:sig:{tc}", b.s_ds)
            for og in range(8):
                b.wtm(sc, f"{pp}f:fue:{og}")
                for tc in range(4):
                    sl = AP(a2t, 4096 * tc + 512 * og, [[16384, 128], [1, 512]])
                    sc.activation(out=AP(b.a3[:].tensor, 0, [[4096, 128], [1, 512]]),
                                  in_=sl, func=AF.Copy,
                                  accum_out=b.sm[:, 8 + 8 * tc + og:9 + 8 * tc + og])
                    sc.activation(out=AP(b.a3[:].tensor, 512, [[4096, 128], [1, 512]]),
                                  in_=sl, func=AF.Square,
                                  accum_out=b.stats[:, 8 * tc + og:8 * tc + og + 1])
            b.mark(f"{pp}f:fsq", b.s_ds)
            b.wtm(sc, f"{pp}f:fvar")
            for tc in range(4):
                sc.activation(out=b.stats[:, 49 + 2 * tc:50 + 2 * tc],
                              in_=b.stats[:, 49 + 2 * tc:50 + 2 * tc], func=AF.Sqrt)
            b.mark(f"{pp}f:fsqrt", b.s_ds)

        @b.block.tensor
        def _(te):
            for tc in range(4):
                b.wtm(te, f"{pp}f:dmat:{tc}")
                if tc == 0:
                    b.wtm(te, f"{pp}o:x2done")
                    b.wtm(te, "pro:wup")
                if tc >= 1:
                    b.wtm(te, f"{pp}f:logit:{tc - 1}")
                for ic in range(NIC):
                    ins = te.matmul(out=b.pb[4][:, 0:8],
                                    lhsT=AP(a1t, 512 * ic + 128 * tc, [[16384, 128], [1, 128]]),
                                    rhs=b.wr_bf[:, 8 * ic:8 * (ic + 1)],
                                    start=(ic == 0), stop=(ic == NIC - 1))
                b.inc(ins, b.s_mm)
                b.mark(f"{pp}f:mmL:{tc}", b.s_mm)
            for og in range(8):
                if og >= 1:
                    b.wtm(te, f"{pp}f:fue:{og - 1}")
                b.wtm(te, f"{pp}f:wbt:{og}")
                for tc in range(4):
                    te.matmul(out=b.pb[tc][:], lhsT=b.onesb[0:1, :],
                              rhs=b.probs[0:1, 512:1024],
                              start=True, stop=False)
                for ic in range(NIC):
                    b.wtm(te, f"{pp}f:wb:{og}:{ic}")
                    wb_ = b.wchunk_buf(og * NIC + ic, 6)
                    for tc in range(4):
                        ins = te.matmul(out=b.pb[tc][:],
                                        lhsT=AP(a1t, 512 * ic + 128 * tc, [[16384, 128], [1, 128]]),
                                        rhs=wb_,
                                        start=False, stop=False)
                    b.inc(ins, b.s_mm)
                    b.mark(f"{pp}f:mmfb:{og}:{ic}", b.s_mm)
                if og == 0:
                    for tc in range(4):
                        b.wtm(te, f"{pp}f:comb:{tc}")
                        if tc >= 2:
                            b.wtm(te, f"{pp}f:ctev:{tc - 2}")
                        psx = b.pf[tc % 2]
                        b.inc(te.transpose(out=psx[0:8, 0:128], in_=b.logitb[:, 8 * tc:8 * (tc + 1)],
                                           identity=b.ident[:]), b.s_mm)
                        b.mark(f"{pp}f:mmct:{tc}", b.s_mm)
                    b.wtm(te, f"{pp}f:combT")
                    b.inc(te.matmul(out=b.pb[5][:], lhsT=b.e8[:], rhs=b.combT[:],
                                    start=True, stop=True), b.s_mm)
                    b.mark(f"{pp}f:mmcb", b.s_mm)
                    b.wtm(te, f"{pp}f:wbt:0")
                    b.wtm(te, f"{pp}f:logit:3")
                    for ic in range(NIC):
                        ins = te.matmul(out=b.pb[4][:],
                                        lhsT=b.wsmall[:, 128 * ic:128 * (ic + 1)],
                                        rhs=AP(a1t, 512 * ic, [[16384, 128], [1, 512]]),
                                        start=(ic == 0), stop=(ic == NIC - 1))
                    b.inc(ins, b.s_mm)
                    b.mark(f"{pp}f:mmh", b.s_mm)

                b.wtm(te, f"{pp}f:hw")
                for tc in range(4):
                    ins = te.matmul(out=b.pb[tc][:], lhsT=b.hw[:, 128 * tc:128 * (tc + 1)],
                                    rhs=b.probs[:, 0:512],
                                    start=False, stop=True)
                b.inc(ins, b.s_mm)
                b.mark(f"{pp}f:mmf:{og}", b.s_mm)

    def emit_taps(self):
        b = self
        @b.block.gpsimd
        def _(g):
            b.wt(g, b.s_dv)
            b.wt(g, b.s_ds)
            b.wt(g, b.s_mm)
            def dump(name, sbt, base, ntile, ncol):
                hb = b.tap_h[name]
                for i in range(ntile):
                    b.inc(g.dma_start(out=hb[128 * i:128 * (i + 1), :],
                                      in_=AP(sbt[:].tensor, base + ncol * i,
                                             [[16384, 128], [1, ncol]])), b.s_st, 16)
            if "kcvcT" in b.taps:
                dump("kcvcT", b.a2, 0, 4, 2048)
            if "latT" in b.taps:
                dump("latT", b.a2, 8192, 4, 2048)
            if "KT" in b.taps:
                dump("KT", b.KT, 0, 8, 2048)
            if "V" in b.taps:
                dump("V", b.V, 0, 16, 1024)
            if "x1T" in b.taps:
                dump("x1T", b.a1, 0, 32, 512)
            if "QT" in b.taps:
                dump("QT", b.a2, 0, 32, 512)
            if "attn_outT" in b.taps:
                dump("attn_outT", b.a1, 0, 32, 512)
            if "B" in b.taps:
                b.inc(g.dma_start(out=b.tap_h["B"][:, 0:1], in_=b.bvec[:],
                                  allow_slow_non_contiguous=True), b.s_st, 16)
                b.inc(g.dma_start(out=b.tap_h["B"][0:1, 1:4], in_=b.sm[0:1, 56:59]), b.s_st, 16)
            if "x2" in b.taps:
                dump("x2", b.x2, 0, 4, 4096)
            if "x3T" in b.taps:
                dump("x3T", b.a1, 0, 32, 512)
            if "cb" in b.taps:
                b.inc(g.dma_start(out=b.tap_h["cb"][:], in_=b.cb[:]), b.s_st, 16)
            if "hw" in b.taps:
                b.inc(g.dma_start(out=b.tap_h["hw"][:], in_=b.hw[:]), b.s_st, 16)
            if "fU" in b.taps:
                dump("fU", b.a2, 0, 4, 4096)
            if "logit" in b.taps:
                b.inc(g.dma_start(out=b.tap_h["logit"][:], in_=b.logit[:]), b.s_st, 16)

    def emit_end(self):
        b = self
        @b.block.sync
        def _(sync):
            b.wt(sync, b.s_st)

    def emit_all(self):
        self.emit_consts()
        self.emit_prologue()
        if self.stop_after != "prologue":
            self.emit_pass(0)
            if self.stop_after == "full":
                self.emit_pass(1)
        if self.taps:
            self.emit_taps()
        self.emit_end()
        return self.finish()


def build_program(taps=(), stop_after="full"):
    rec = Build(taps=taps, stop_after=stop_after)
    rec.emit_all()
    return Build(taps=taps, stop_after=stop_after, marks=rec.marks).emit_all()


def make_masks(part):
    perm = PERMS[part]
    out = []
    for P, qb in enumerate(([0, 3] if part == 0 else [1, 2])):
        NC = NCP[P]
        m = np.zeros((128, 4096), np.float32)
        k = np.arange(128)[:, None]
        q = np.arange(512)[None, :]
        pa = np.asarray(perm)
        for mj in range(8):
            kc = NC - 8 + mj
            g = 2 * kc + k // 64
            ktok = 64 * (4 * (g // 4) + pa[g % 4]) + (k % 64)
            qtok = 512 * qb + q
            m[:, 512 * mj:512 * (mj + 1)] = (ktok <= qtok).astype(np.float32)
        out.append(m.astype(NPBF))
    return out


def _prep_shared(inputs):
    f32 = np.float32
    g = {}
    g["wq_t"] = np.ascontiguousarray(np.asarray(inputs["L0_attn_Wq"], f32).T)
    wkv = np.concatenate([np.asarray(inputs["L0_attn_Wk_down"], f32),
                          np.asarray(inputs["L0_attn_Wv_down"], f32)], axis=0)
    g["wkv_t"] = np.ascontiguousarray(wkv.T)
    g["wku_t"] = np.ascontiguousarray(np.asarray(inputs["L0_attn_Wk_up"], f32).T)
    g["wvu_t"] = np.ascontiguousarray(np.asarray(inputs["L0_attn_Wv_up"], f32).T)
    g["wo_t"] = np.ascontiguousarray(np.asarray(inputs["L0_attn_Wo"], f32).T)
    g["wb_t"] = np.ascontiguousarray(np.asarray(inputs["L0_ffn_W_base"], f32).T)
    g["at_t"] = np.ascontiguousarray(np.asarray(inputs["L0_ffn_A"], f32).reshape(128, D).T)
    g["bt_t"] = np.ascontiguousarray(
        np.asarray(inputs["L0_ffn_B"], f32).transpose(0, 2, 1).reshape(128, D))
    g["wr_t"] = np.ascontiguousarray(np.asarray(inputs["L0_ffn_W_router"], f32).T)
    g["bb"] = np.ascontiguousarray(np.asarray(inputs["L0_ffn_b_base"], f32))
    g["cst_ident"] = np.eye(128, dtype=NPBF)
    e8 = np.zeros((8, 128), np.float32)
    for e in range(8):
        e8[e, 16 * e:16 * (e + 1)] = 2.0
    g["cst_e8"] = e8.astype(NPBF)
    g["cst_ones_f32"] = np.ones((128, 128), np.float32)
    g["cst_ones_bf"] = np.ones((128, 128), NPBF)
    return g


def make_in_maps(inputs):
    x = np.asarray(inputs["L0_attn_hidden_states"], np.float32)
    g = _prep_shared(inputs)
    masks = [make_masks(0), make_masks(1)]
    in_maps = []
    for c in range(8):
        bb, p = c // 2, c % 2
        m = dict(g)
        perm = PERMS[p]
        m["x"] = np.ascontiguousarray(
            np.concatenate([x[bb, 512 * blk:512 * (blk + 1)] for blk in perm], axis=0))
        m["cst_masksA"] = masks[p][0]
        m["cst_masksB"] = masks[p][1]
        in_maps.append(m)
    return in_maps


def kernel(**inputs):
    in_maps = make_in_maps(inputs)
    nc = build_program()
    res = run_bass_kernel_spmd(nc, in_maps, core_ids=list(range(8)))
    out = np.empty((4, S, D), np.float32)
    for c in range(8):
        bb, p = c // 2, c % 2
        qbs = [0, 3] if p == 0 else [1, 2]
        r = res.results[c]["out"]
        out[bb, 512 * qbs[0]:512 * (qbs[0] + 1)] = r[0:512]
        out[bb, 512 * qbs[1]:512 * (qbs[1] + 1)] = r[512:1024]
    return out
